# version 11
# baseline (speedup 1.0000x reference)
from collections import defaultdict
from contextlib import ExitStack

import numpy as np

import concourse.bass as bass
import concourse.mybir as mybir
from concourse.bass_utils import run_bass_kernel_spmd

F32 = mybir.dt.float32
BF16 = mybir.dt.bfloat16
AF = mybir.ActivationFunctionType
ALU = mybir.AluOpType
AX = mybir.AxisListType

D = 1024
NH = 16
DK = 64
NE = 8
HE = 64
THETA = 10000.0
EPS = 1e-5
CSHIFT = 24.0
NCORES = 8
DC = D // 128

PERM64 = []
for _blk in range(2):
    PERM64 += [2 * (16 * _blk + j) for j in range(16)]
    PERM64 += [2 * (16 * _blk + j) + 1 for j in range(16)]
PERM64 = np.array(PERM64)
SHUF16 = list(range(16, 32)) + list(range(16))


class Sem:
    def __init__(self, stack, nc, name):
        self.h = stack.enter_context(nc.semaphore(name))
        self.n = 0

    def inc(self, amt=1):
        self.n += amt
        return self.n


class Builder:
    ENGS = ("sync", "tensor", "scalar", "vector", "gpsimd")

    def __init__(self, B_core, S):
        assert S % 512 == 0
        self.BC, self.S = B_core, S
        self.NT = S // 128
        self.NQB = S // 512
        self.nc = bass.Bass("TRN2", target_bir_lowering=False)
        self.plan = defaultdict(list)
        self.st = ExitStack()
        self.gpair = 0
        self.ghqb = 0
        self.pt_free = defaultdict(int)
        self.pss_free = defaultdict(int)

    def add(self, eng, fn):
        self.plan[eng].append(fn)

    def flush(self):
        plan, self.plan = self.plan, defaultdict(list)
        if not any(plan.values()):
            return
        with self.nc.Block() as blk:
            for eng in self.ENGS:
                if plan[eng]:
                    getattr(blk, eng)(
                        lambda e, fns=tuple(plan[eng]): [f(e) for f in fns])

    def warm(self, psum_ap, n):
        def _w(e, n=n, p=psum_ap):
            for _ in range(n):
                e.matmul(p, self.ident[:, 0:128], self.ident[:, 128:640],
                         start=True, stop=True, skip_group_check=True)
        self.add("tensor", _w)

    def sb(self, stack, name, shape, dtype, side="left"):
        return stack.enter_context(
            self.nc.sbuf_tensor(name, shape, dtype, side=side))

    def ps(self, stack, name, shape, dtype):
        return stack.enter_context(self.nc.psum_tensor(name, shape, dtype))

    def build(self):
        nc, st = self.nc, self.st
        S, BC = self.S, self.BC

        self.x_hbm = nc.dram_tensor("x", [BC, S, D], F32, kind="ExternalInput").ap()
        self.out_hbm = nc.dram_tensor("out", [BC, S, D], F32,
                                      kind="ExternalOutput").ap()
        dram_in = [
            ("wqk", [NH, DC, 128, 128]), ("wv", [DC, 128, D]), ("wo", [DC, 128, D]),
            ("wd", [128, NE * DC * HE]), ("wup", [HE, NE * D]),
            ("ropeA", [128, S]), ("ropeB", [128, S]),
            ("gtile", [128, D]), ("btile", [128, D]), ("ones", [128, 128]),
            ("crow", [1, 6 * D]),
            ("wr", [128, DC * NE]), ("br", [1, NE]),
            ("bdall", [HE, NE]), ("bupall", [NE, D]), ("ecol", [128, 2]),
        ]
        self.din = {n: nc.dram_tensor(n, sh, F32, kind="ExternalInput").ap()
                    for n, sh in dram_in}
        self.din["ident"] = nc.dram_tensor("ident", [128, 640], BF16,
                                           kind="ExternalInput").ap()
        self.wqk_b = nc.dram_tensor("wqk_b", [NH, DC, 128, 128], BF16).ap()
        self.wv_b = nc.dram_tensor("wv_b", [DC, 128, D], BF16).ap()
        self.wo_b = nc.dram_tensor("wo_b", [DC, 128, D], BF16).ap()
        self.wd_b = nc.dram_tensor("wd_b", [128, NE * DC * HE], BF16).ap()
        self.wup_b = nc.dram_tensor("wup_b", [HE, NE * D], BF16).ap()

        with st:
            for name in ("c0 w0 ldx ldw ldw2 stat rstd u1 tr u1T qk rope qkT "
                         "vmm vev smm exp msk pvmm lrow bc norm womm x1 st ch w0b st2 "
                         "ldxb ldwb ldw3 ldw3b cha chg "
                         "t3 zmm z ztr zT gat sel hrelu ymm t2 x2 misc macc"
                         ).split():
                setattr(self, "s_" + name, Sem(st, nc, name))

            self.ropeA = self.sb(st, "ropeA_s", [128, S], F32)
            self.ropeB = self.sb(st, "ropeB_s", [128, S], F32)
            self.gtile = self.sb(st, "gtile_s", [128, D], F32)
            self.btile = self.sb(st, "btile_s", [128, D], F32)
            self.ident = self.sb(st, "ident_s", [128, 640], BF16)
            self.ones = self.sb(st, "ones_s", [128, 128], F32)
            self.wrsb = self.sb(st, "wr_s", [128, DC * NE], F32)
            self.brsb = self.sb(st, "br_s", [1, NE], F32)
            self.bdall = self.sb(st, "bd_s", [HE, NE], F32)
            self.bupall = self.sb(st, "bup_s", [NE, D], F32)
            self.alpha = self.sb(st, "alpha_s", [1, D], F32)
            self.ecol = self.sb(st, "ecol_s", [128, 2], F32)

            self.phase0()
            for s in range(BC):
                self.seq(s)

            self.add("sync", lambda e, w=self.s_st.n, w2=self.s_st2.n: (
                e.wait_ge(self.s_st.h, w), e.wait_ge(self.s_st2.h, w2)))
            self.flush()
        return nc

    def phase0(self):
        consts = [
            (self.ropeA, "ropeA"), (self.ropeB, "ropeB"),
            (self.gtile, "gtile"), (self.btile, "btile"),
            (self.ident, "ident"), (self.ones, "ones"),
            (self.wrsb, "wr"), (self.brsb, "br"),
            (self.bdall, "bdall"), (self.bupall, "bupall"),
            (self.ecol, "ecol"),
        ]
        for dst, name in consts:
            src = self.din[name]
            self.add("sync", lambda e, a=dst, b=src:
                     e.dma_start(out=a[:, :], in_=b[:, :]).then_inc(self.s_c0.h, 16))
            self.s_c0.inc(16)


        casts = []
        for h in range(NH):
            casts.append((self.din["wqk"][h].rearrange("c d e -> d c e"),
                          self.wqk_b[h].rearrange("c d e -> d c e"),
                          128, DC * 128))
        for c in range(DC):
            casts.append((self.din["wv"][c], self.wv_b[c], 128, D))
            casts.append((self.din["wo"][c], self.wo_b[c], 128, D))
        for q in range(4):
            n = NE * DC * HE // 4
            casts.append((self.din["wd"][:, q * n:(q + 1) * n],
                          self.wd_b[:, q * n:(q + 1) * n], 128, n))
        for q in range(4):
            n = NE * D // 4
            casts.append((self.din["wup"][:, q * n:(q + 1) * n],
                          self.wup_b[:, q * n:(q + 1) * n], HE, n))

        with ExitStack() as sc:
            cbuf_f = self.sb(sc, "cbuf_f", [128, 4096], F32)
            cbuf_b = self.sb(sc, "cbuf_b", [128, 4096], BF16)
            cvg = self.sb(sc, "cvg", [1, 2 * D], F32)
            self.add("sync", lambda e: e.dma_start(
                out=cvg[:, :], in_=self.din["crow"][:, 4 * D:6 * D])
                .then_inc(self.s_c0.h, 16))
            cw = self.s_c0.inc(16)

            c1 = self.s_cha.inc()

            def _alpha_act(e, w=cw, c1=c1):
                e.wait_ge(self.s_c0.h, w)
                e.activation(self.alpha[0:1, :], cvg[0:1, 0:D], AF.Ln,
                             bias=self.ecol[0:1, 0:1]).then_inc(self.s_cha.h, 1)
                e.wait_ge(self.s_cha.h, c1)
                e.activation(self.alpha[0:1, :], self.alpha[0:1, :], AF.Exp,
                             scale=-0.5).then_inc(self.s_misc.h, 1)
            self.add("scalar", _alpha_act)
            wm = self.s_misc.inc()

            def _alpha_dve(e, w=wm):
                e.wait_ge(self.s_misc.h, w)
                e.tensor_mul(self.alpha[0:1, :], self.alpha[0:1, :],
                             cvg[0:1, D:2 * D]).then_inc(self.s_misc.h, 1)
            self.add("vector", _alpha_dve)
            self.s_misc.inc()
            wsem = [self.s_w0, self.s_w0b]
            for i, (src, dst, npart, nf) in enumerate(casts):
                assert nf <= 2048
                off = (i % 2) * 2048
                bf = cbuf_f[0:npart, off:off + nf]
                bb = cbuf_b[0:npart, off:off + nf]
                if len(src.shape) == 3:
                    bf = bf.rearrange("p (c e) -> p c e", e=src.shape[2])
                    bb = bb.rearrange("p (c e) -> p c e", e=src.shape[2])
                ws = wsem[i % 2]
                w_st = ws.n if i >= 2 else 0
                w_ld = self.s_ldw.inc(16)
                self.add("sync", lambda e, s=src, bf=bf, ws=ws, w=w_st: (
                    e.wait_ge(ws.h, w) if w else None,
                    e.dma_start(out=bf, in_=s).then_inc(self.s_ldw.h, 16)))
                v_cast = self.s_macc.inc()
                self.add("vector", lambda e, bf=bf, bb=bb, ws=ws, w=w_ld, w2=w_st: (
                    e.wait_ge(self.s_ldw.h, w),
                    e.wait_ge(ws.h, w2) if w2 else None,
                    e.tensor_copy(bb, bf).then_inc(self.s_macc.h, 1)))
                self.add("sync", lambda e, d=dst, bb=bb, ws=ws, w=v_cast: (
                    e.wait_ge(self.s_macc.h, w),
                    e.dma_start(out=d, in_=bb).then_inc(ws.h, 16)))
                ws.inc(16)
            self.add("sync", lambda e, w=self.s_w0.n, w2=self.s_w0b.n: (
                e.wait_ge(self.s_w0.h, w), e.wait_ge(self.s_w0b.h, w2)))
            self.add("gpsimd", lambda e, w=self.s_c0.n: e.wait_ge(self.s_c0.h, w))
            self.add("vector", lambda e, w=self.s_misc.n: e.wait_ge(self.s_misc.h, w))
            self.flush()

    def seq(self, s):
        S, NT = self.S, self.NT
        scA = ExitStack()
        scQ = ExitStack()
        try:
            self.psq = [self.ps(scQ, f"psq{s}_{i}", [128, 512], F32)
                        for i in range(2)]
            with ExitStack() as sc1:
                qkT = self.sb(sc1, f"qkT{s}", [128, NH * S], BF16)
                vsb = self.sb(sc1, f"v{s}", [128, NT * NH * 66], BF16)
                self.p1(s, sc1, qkT, vsb)
                scQ.close()
                attnT = self.sb(scA, f"attnT{s}", [128, DC * S], BF16,
                                side="right")
                self.p2a(s, sc1, qkT, vsb, attnT)
            with ExitStack() as sc2:
                x1 = self.sb(sc2, f"x1_{s}", [128, NT * D], F32)
                self.p2b(s, sc2, attnT, x1)
                scA.close()
                self.p3(s, sc2, x1)
            self.flush()
        finally:
            scA.close()
            scQ.close()

    def p1(self, s, sc1, qkT, vsb):
        S, NT, NQB = self.S, self.NT, self.NQB
        with ExitStack() as sc:
            u1T = self.sb(sc, f"u1T_{s}", [128, DC * S], BF16)
            wvb = self.sb(sc, f"wvb{s}", [128, DC * D], BF16)

            v_vset = self.s_misc.inc()
            self.add("gpsimd", lambda e: e.memset(vsb[:, :], 1.0)
                     .then_inc(self.s_misc.h, 1))
            for c in range(DC):
                self.add("sync", lambda e, c=c: e.dma_start(
                    out=wvb[:, c * D:(c + 1) * D], in_=self.wv_b[c])
                    .then_inc(self.s_ldw2.h, 16))
                self.s_ldw2.inc(16)
            v_wv = self.s_ldw2.n

            with ExitStack() as sca:
                xbuf = [self.sb(sca, f"xb{s}_{i}", [128, D], F32) for i in range(2)]
                u1buf = [self.sb(sca, f"u1b{s}_{i}", [128, D], BF16) for i in range(2)]
                stat6 = [self.sb(sca, f"st6_{s}_{i}", [128, 12], F32)
                         for i in range(2)]
                mv = [self.sb(sca, f"mv_{s}_{i}", [128, 2], F32) for i in range(2)]
                rstd = [self.sb(sca, f"rs_{s}_{i}", [128, 1], F32)
                        for i in range(2)]
                ptr = [self.ps(sca, f"ptr{s}_0", [128, 1024], BF16)] * 2
                psv = [self.ps(sca, f"psv{s}_0", [128, 1024], F32)] * 2
                b_u1 = self.s_u1.n
                b_tr = self.s_tr.n
                b_ev = self.s_u1T.n
                b_stat = self.s_stat.n
                b_rstd = self.s_rstd.n
                b_vev = self.s_vev.n

                def st_load(t):
                    sl = t % 2
                    lsem = self.s_ldx if sl == 0 else self.s_ldxb
                    w_u1 = b_u1 + t - 1 if t >= 2 else 0
                    w_self = lsem.n if t < 2 else 0
                    wst = (self.s_st.n, self.s_st2.n) if t < 2 else (0, 0)
                    lsem.inc(16)
                    self.add("sync", lambda e, s_=s, t=t, sl=sl, ls=lsem,
                             w=w_u1, wsf=w_self, wst=wst: (
                        e.wait_ge(self.s_u1.h, w) if w else None,
                        e.wait_ge(ls.h, wsf) if wsf else None,
                        e.wait_ge(self.s_st.h, wst[0]) if wst[0] else None,
                        e.wait_ge(self.s_st2.h, wst[1]) if wst[1] else None,
                        e.dma_start(out=xbuf[sl][:, :],
                                    in_=self.x_hbm[s_, t * 128:(t + 1) * 128, :])
                         .then_inc(ls.h, 16)))

                def st_stats(t):
                    sl = t % 2
                    lsem = self.s_ldx if sl == 0 else self.s_ldxb
                    w_ld = lsem.n
                    c1 = self.s_ch.inc()
                    self.s_stat.inc()
                    self.add("vector", lambda e, sl=sl, ls=lsem, w=w_ld, c1=c1: (
                        e.wait_ge(ls.h, w),
                        e.bn_stats(stat6[sl][:, 0:6], xbuf[sl][:, 0:512]),
                        e.bn_stats(stat6[sl][:, 6:12], xbuf[sl][:, 512:1024])
                         .then_inc(self.s_ch.h, 1),
                        e.wait_ge(self.s_ch.h, c1),
                        e.bn_aggr(mv[sl][:, :], stat6[sl][:, :])
                         .then_inc(self.s_stat.h, 1)))

                def st_rstd(t):
                    sl = t % 2
                    c1 = self.s_cha.inc()
                    self.s_rstd.inc()
                    self.add("scalar", lambda e, sl=sl, w=b_stat + t + 1, c1=c1: (
                        e.wait_ge(self.s_stat.h, w),
                        e.activation(rstd[sl][:, :], mv[sl][:, 1:2], AF.Ln,
                                     bias=self.ecol[:, 0:1])
                         .then_inc(self.s_cha.h, 1),
                        e.wait_ge(self.s_cha.h, c1),
                        e.activation(rstd[sl][:, :], rstd[sl][:, :], AF.Exp,
                                     scale=-0.5).then_inc(self.s_rstd.h, 1)))

                def st_apply(t):
                    sl = t % 2
                    w_tr = b_tr + t - 1 if t >= 2 else 0
                    self.s_u1.inc()
                    self.add("vector", lambda e, sl=sl, w=b_rstd + t + 1,
                             w2=w_tr: (
                        e.wait_ge(self.s_rstd.h, w),
                        e.wait_ge(self.s_tr.h, w2) if w2 else None,
                        e.tensor_scalar(u1buf[sl][:, :], xbuf[sl][:, :],
                                        mv[sl][:, 0:1], rstd[sl][:, 0:1],
                                        ALU.subtract, ALU.mult)
                         .then_inc(self.s_u1.h, 1)))

                def st_tr(t):
                    sl = t % 2
                    w_ev = b_ev + t if t >= 1 else 0
                    self.s_tr.inc()

                    def _tr(e, sl=sl, w=b_u1 + t + 1, w2=w_ev):
                        e.wait_ge(self.s_u1.h, w)
                        if w2:
                            e.wait_ge(self.s_u1T.h, w2)
                        for c in range(DC):
                            ins = e.transpose(ptr[sl][:, c * 128:(c + 1) * 128],
                                              u1buf[sl][:, c * 128:(c + 1) * 128],
                                              self.ident[:, 0:128])
                        ins.then_inc(self.s_tr.h, 1)
                    self.add("tensor", _tr)

                def st_evac(t):
                    sl = t % 2
                    self.s_u1T.inc()
                    dstap = (u1T[:, :].rearrange("p (c q) -> p c q", q=S)
                             [:, :, t * 128:(t + 1) * 128])
                    srcap = ptr[sl][:, :].rearrange("p (c f) -> p c f", f=128)
                    self.add("vector", lambda e, d=dstap, sr=srcap,
                             w=b_tr + t + 1: (
                        e.wait_ge(self.s_tr.h, w),
                        e.tensor_copy(d, sr).then_inc(self.s_u1T.h, 1)))

                def plan_v(t):
                    sl = t % 2
                    w_ev2 = b_vev + t if t >= 1 else 0
                    w_u1T = b_ev + t + 1

                    def _vmm(e, t=t, sl=sl, w=v_wv, w2=w_ev2, wu=w_u1T,
                             first=(t == 0)):
                        if first:
                            e.wait_ge(self.s_ldw2.h, w)
                        e.wait_ge(self.s_u1T.h, wu)
                        if w2:
                            e.wait_ge(self.s_vev.h, w2)
                        for c in range(DC):
                            for nb in range(2):
                                ins = e.matmul(
                                    psv[sl][:, nb * 512:(nb + 1) * 512],
                                    u1T[:, c * S + t * 128: c * S + (t + 1) * 128],
                                    wvb[:, c * D + nb * 512: c * D + (nb + 1) * 512],
                                    start=(c == 0), stop=(c == DC - 1),
                                    skip_group_check=True)
                        ins.then_inc(self.s_vmm.h, 1)
                    self.add("tensor", _vmm)
                    v_vmm = self.s_vmm.inc()
                    dstap = (vsb[:, t * NH * 66:(t + 1) * NH * 66]
                             .rearrange("p (h c) -> p h c", c=66)[:, :, 0:64])
                    srcap = psv[sl][:, :].rearrange("p (h c) -> p h c", c=64)
                    wv_first = v_vset if t == 0 else 0
                    self.add("vector", lambda e, d=dstap, sr=srcap, w=v_vmm,
                             w2=wv_first: (
                        e.wait_ge(self.s_misc.h, w2) if w2 else None,
                        e.wait_ge(self.s_vmm.h, w),
                        e.tensor_copy(d, sr).then_inc(self.s_vev.h, 1)))
                    self.s_vev.inc()

                for t in range(NT):
                    st_load(t)
                    st_stats(t)
                    st_rstd(t)
                    if t >= 1:
                        st_apply(t - 1)
                        st_tr(t - 1)
                        st_evac(t - 1)
                    if t >= 2:
                        plan_v(t - 2)
                st_apply(NT - 1)
                st_tr(NT - 1)
                st_evac(NT - 1)
                plan_v(NT - 2)
                plan_v(NT - 1)
                u1T_done = self.s_u1T.n

            with ExitStack() as scb:
                wqkb = [self.sb(scb, f"wqk{s}_{i}", [128, DC * 128], BF16)
                        for i in range(2)]
                qks = [self.sb(scb, f"qks{s}_{i}", [128, 512], F32)
                       for i in range(2)]
                t1b = [self.sb(scb, f"t1b{s}_{i}", [128, 512], F32)
                       for i in range(2)]
                t2b = [self.sb(scb, f"t2b{s}_{i}", [128, 512], F32)
                       for i in range(2)]
                psq = self.psq
                self.qkT_base = self.s_qkT.n
                b_add = self.s_macc.n
                self.rope_done = (self.s_macc, b_add + NH * self.NQB)
                self.vev_base = b_vev
                b_qk = self.s_qk.n
                b_rope = self.s_rope.n
                b_qkT = self.s_qkT.n
                self.add("tensor", lambda e, w=u1T_done: e.wait_ge(self.s_u1T.h, w))
                for h in range(NH):
                    wsem = self.s_ldw if h % 2 == 0 else self.s_ldwb
                    w_q = b_qk + (h - 1) * NQB if h >= 2 else 0
                    w_self = wsem.n if h < 2 else 0
                    v_ldw = wsem.inc(16)
                    self.add("sync", lambda e, h=h, ws=wsem, w=w_q, wsf=w_self,
                             wu=u1T_done if h < 2 else 0: (
                        e.wait_ge(self.s_qk.h, w) if w else None,
                        e.wait_ge(ws.h, wsf) if wsf else None,
                        e.wait_ge(self.s_u1T.h, wu) if wu else None,
                        e.dma_start(out=wqkb[h % 2][:, :]
                                    .rearrange("d (c e) -> d c e", e=128),
                                    in_=self.wqk_b[h].rearrange("c d e -> d c e"))
                         .then_inc(ws.h, 16)))
                    for qb in range(NQB):
                        i = h * NQB + qb
                        w_ps = b_rope + i - 1 if i >= 2 else 0

                        wqsem = self.s_ldw if h % 2 == 0 else self.s_ldwb

                        def _qkmm(e, h=h, qb=qb, sl=i % 2, ws=wqsem,
                                  wl=v_ldw, wp=w_ps):
                            e.wait_ge(ws.h, wl)
                            if wp:
                                e.wait_ge(self.s_rope.h, wp)
                            for c in range(DC):
                                ins = e.matmul(
                                    psq[sl][:, :],
                                    wqkb[h % 2][:, c * 128:(c + 1) * 128],
                                    u1T[:, c * S + qb * 512: c * S + qb * 512 + 512],
                                    start=(c == 0), stop=(c == DC - 1))
                            ins.then_inc(self.s_qk.h, 1)
                        self.add("tensor", _qkmm)
                        v_qk = self.s_qk.inc()
                        w_gp = b_qkT + i - 1 if i >= 2 else 0
                        self.add("vector", lambda e, sl=i % 2, qb=qb, w=v_qk, w2=w_gp: (
                            e.wait_ge(self.s_qk.h, w),
                            e.wait_ge(self.s_qkT.h, w2) if w2 else None,
                            e.stream_shuffle(qks[sl][:, :], psq[sl][:, :], SHUF16),
                            e.tensor_mul(t1b[sl][:, :], psq[sl][:, :],
                                         self.ropeA[:, qb * 512:qb * 512 + 512])
                             .then_inc(self.s_rope.h, 1)))
                        v_rope = self.s_rope.inc()

                        w_t2 = b_add + i - 1 if i >= 2 else 0

                        def _gp(e, qb=qb, sl=i % 2, w=v_rope, wo=w_t2):
                            e.wait_ge(self.s_rope.h, w)
                            if wo:
                                e.wait_ge(self.s_macc.h, wo)
                            e.tensor_mul(t2b[sl][:, :], qks[sl][:, :],
                                         self.ropeB[:, qb * 512:qb * 512 + 512]) \
                             .then_inc(self.s_qkT.h, 1)
                        self.add("gpsimd", _gp)
                        v_gp = self.s_qkT.inc()
                        self.add("vector", lambda e, h=h, qb=qb, sl=i % 2,
                                 w=v_gp: (
                            e.wait_ge(self.s_qkT.h, w),
                            e.tensor_add(qkT[:, h * S + qb * 512:
                                             h * S + qb * 512 + 512],
                                         t1b[sl][:, :], t2b[sl][:, :])
                             .then_inc(self.s_macc.h, 1)))
                        self.s_macc.inc()

    def p2a(self, s, sc1, qkT, vsb, attnT):
        S, NQB = self.S, self.NQB
        with ExitStack() as sc:
            PT = self.sb(sc, f"PT{s}", [128, 8 * 512], BF16)
            lrow = [self.sb(sc, f"lr{s}_{i}", [1, 512], F32) for i in range(2)]
            rb = [self.sb(sc, f"rb{s}_{i}", [64, 512], F32) for i in range(2)]
            pss6 = self.ps(sc, f"pss{s}", [128, 3072], F32)
            pss = [pss6[:, i * 1024:(i + 1) * 1024] for i in range(3)]
            posb = [self.ps(sc, f"po{s}_{i}", [128, 512], F32) for i in range(2)]
            pso = posb

            self.add("tensor", lambda e, rd=self.rope_done:
                     e.wait_ge(rd[0].h, rd[1]))
            self.warm(pss6[:, 1024:1536], 16)
            pending = None

            def plan_pv(h, qb, i_hqb, K, exp_done, msk_done, js):
                for j in js:
                    sigma = max(0, j - 4 * qb) * 128
                    sl4 = j % 8
                    wmask = msk_done.get(j, 0)
                    w_exp = exp_done[j // 2]

                    wv = self.vev_base + j + 1

                    def _pv(e, h=h, qb=qb, j=j, K=K, sl4=sl4, sigma=sigma,
                            we=w_exp, wm=wmask, i_hqb=i_hqb, wv=wv):
                        e.wait_ge(self.s_vev.h, wv)
                        e.wait_ge(self.s_exp.h, we)
                        if wm:
                            e.wait_ge(self.s_msk.h, wm)
                        if j == 0 and i_hqb >= 2:
                            e.wait_ge(self.s_norm.h, i_hqb - 1)
                        e.matmul(pso[i_hqb % 2][0:65, sigma:512],
                                 vsb[:, j * NH * 66 + h * 66:
                                     j * NH * 66 + h * 66 + 65],
                                 PT[:, sl4 * 512 + sigma: sl4 * 512 + 512],
                                 start=(j == 0), stop=(j == K - 1),
                                 skip_group_check=True) \
                         .then_inc(self.s_pvmm.h, 1)
                    self.add("tensor", _pv)
                    self.s_pvmm.inc()
                    self.pt_free[j % 8] = self.s_pvmm.n

            def plan_tail(h, qb, i_hqb):
                w_pv = self.s_pvmm.n
                self.add("scalar", lambda e, sl=i_hqb % 2, w=w_pv: (
                    e.wait_ge(self.s_pvmm.h, w),
                    e.copy(lrow[sl][:, :], pso[sl][64:65, :])
                     .then_inc(self.s_lrow.h, 1)))
                v_l = self.s_lrow.inc()
                w_rec = self.s_norm.n - 1 if i_hqb >= 2 else 0
                self.add("tensor", lambda e, sl=i_hqb % 2, w=v_l, w2=w_rec: (
                    e.wait_ge(self.s_lrow.h, w),
                    e.wait_ge(self.s_norm.h, w2) if w2 else None,
                    e.matmul(posb[sl][64:128, :], self.ones[0:1, 0:64],
                             lrow[sl][:, :], start=True, stop=True,
                             skip_group_check=True)
                     .then_inc(self.s_bc.h, 1)))
                v_bc = self.s_bc.inc()
                dst = attnT[(h % 2) * 64:(h % 2) * 64 + 64,
                            (h // 2) * S + qb * 512:(h // 2) * S + qb * 512 + 512]
                c1 = self.s_ch.inc()
                self.add("vector", lambda e, sl=i_hqb % 2, d=dst, w=v_bc, c1=c1: (
                    e.wait_ge(self.s_bc.h, w),
                    e.reciprocal(rb[sl][:, :], posb[sl][64:128, :])
                     .then_inc(self.s_ch.h, 1),
                    e.wait_ge(self.s_ch.h, c1),
                    e.tensor_mul(d, pso[sl][0:64, :], rb[sl][:, :])
                     .then_inc(self.s_norm.h, 1)))
                self.s_norm.inc()

            for h in range(NH):
                for qb in range(NQB):
                    K = 4 * qb + 4
                    i_hqb = self.ghqb
                    self.ghqb += 1
                    exp_done = {}
                    msk_done = {}
                    for pl0 in range(0, K // 2, 2):
                        group = []
                        for pl in (pl0, pl0 + 1):
                            j0, j1 = 2 * pl, 2 * pl + 1
                            gp = self.gpair
                            self.gpair += 1
                            sl2 = gp % 3
                            w_free = self.pss_free[sl2]
                            qcol = (2 * (h // 2)) * S
                            kcol = (2 * (h // 2) + 1) * S
                            hb = 64 * (h % 2)
                            w_head = 0

                            def _smm(e, qb=qb, j0=j0, sl2=sl2, wf=w_free,
                                     qcol=qcol, kcol=kcol, hb=hb, wh=w_head):
                                if wh:
                                    e.wait_ge(self.s_qkT.h, wh)
                                if wf:
                                    e.wait_ge(self.s_exp.h, wf)
                                for jj in (j0, j0 + 1):
                                    e.matmul(pss[sl2][:, (jj % 2) * 512:
                                                      (jj % 2) * 512 + 512],
                                             qkT[hb:hb + 64, kcol + jj * 128:
                                                 kcol + (jj + 1) * 128],
                                             qkT[hb:hb + 64, qcol + qb * 512:
                                                 qcol + qb * 512 + 512],
                                             start=True, stop=True) \
                                     .then_inc(self.s_smm.h, 1)
                            self.add("tensor", _smm)
                            self.s_smm.inc(2)
                            group.append((pl, j0, j1, sl2, self.s_smm.n))

                        if pending is not None:
                            for (ph, pqb, pi, pK, pexp, pmsk, pjs) in pending:
                                plan_pv(ph, pqb, pi, pK, pexp, pmsk, pjs)
                                if pjs[-1] == pK - 1:
                                    plan_tail(ph, pqb, pi)

                        newpend = []
                        for (pl, j0, j1, sl2, w_smm) in group:
                            slot0 = j0 % 8
                            w_pt = max(self.pt_free[slot0],
                                       self.pt_free[slot0 + 1])

                            def _exp(e, sl2=sl2, slot0=slot0, w=w_smm, wp=w_pt):
                                e.wait_ge(self.s_smm.h, w)
                                if wp:
                                    e.wait_ge(self.s_pvmm.h, wp)
                                e.activation(PT[:, slot0 * 512:
                                                slot0 * 512 + 1024],
                                             pss[sl2][:, :], AF.Exp,
                                             bias=self.ecol[:, 1:2],
                                             scale=0.125) \
                                 .then_inc(self.s_exp.h, 1)
                            self.add("scalar", _exp)
                            v_exp = self.s_exp.inc()
                            self.pss_free[sl2] = v_exp
                            exp_done[pl] = v_exp
                            for j in (j0, j1):
                                if j >= 4 * qb:
                                    sigma = (j - 4 * qb) * 128
                                    col = (j % 8) * 512 + sigma

                                    def _msk(e, col=col, w=v_exp):
                                        e.wait_ge(self.s_exp.h, w)
                                        e.affine_select(
                                            PT[:, col:col + 128],
                                            PT[:, col:col + 128],
                                            pattern=[[1, 128]],
                                            compare_op=ALU.is_ge, fill=0.0,
                                            base=0, channel_multiplier=-1) \
                                         .then_inc(self.s_msk.h, 1)
                                    self.add("gpsimd", _msk)
                                    msk_done[j] = self.s_msk.inc()
                            newpend.append((h, qb, i_hqb, K, exp_done,
                                            msk_done, [j0, j1]))
                        pending = newpend
            if pending is not None:
                for (ph, pqb, pi, pK, pexp, pmsk, pjs) in pending:
                    plan_pv(ph, pqb, pi, pK, pexp, pmsk, pjs)
                    if pjs[-1] == pK - 1:
                        plan_tail(ph, pqb, pi)
            self.attnT_done = self.s_norm.n

    def p2b(self, s, sc2, attnT, x1):
        S, NT = self.S, self.NT
        with ExitStack() as sc:
            wob = self.sb(sc, f"wob{s}", [128, DC * D], BF16)
            xr = [self.sb(sc, f"xr{s}_{i}", [128, D], F32) for i in range(2)]
            psw = [self.ps(sc, f"psw{s}_{i}", [128, 1024], F32) for i in range(2)]
            pv_all = self.s_pvmm.n
            for c in range(DC):
                self.add("sync", lambda e, c=c, wp=pv_all if c == 0 else 0: (
                    e.wait_ge(self.s_pvmm.h, wp) if wp else None,
                    e.dma_start(out=wob[:, c * D:(c + 1) * D], in_=self.wo_b[c])
                    .then_inc(self.s_ldw2.h, 16)))
                self.s_ldw2.inc(16)
            v_wo = self.s_ldw2.n
            b_x1 = self.s_x1.n
            for t in range(NT):
                sl = t % 2
                lsem = self.s_ldx if sl == 0 else self.s_ldxb
                w_x1 = b_x1 + t - 1 if t >= 2 else 0
                w_self = lsem.n if t < 2 else 0
                v_ldx = lsem.inc(16)
                self.add("sync", lambda e, s_=s, t=t, sl=sl, ls=lsem,
                         w=w_x1, wsf=w_self: (
                    e.wait_ge(self.s_x1.h, w) if w else None,
                    e.wait_ge(ls.h, wsf) if wsf else None,
                    e.dma_start(out=xr[sl][:, :],
                                in_=self.x_hbm[s_, t * 128:(t + 1) * 128, :])
                     .then_inc(ls.h, 16)))

                def _womm(e, t=t, sl=sl, w=v_wo, w2=w_x1,
                          wa=self.attnT_done, first=(t == 0)):
                    if first:
                        e.wait_ge(self.s_ldw2.h, w)
                        e.wait_ge(self.s_norm.h, wa)
                    if w2:
                        e.wait_ge(self.s_x1.h, w2)
                    for c in range(DC):
                        for nb in range(2):
                            ins = e.matmul(
                                psw[sl][:, nb * 512:(nb + 1) * 512],
                                attnT[:, c * S + t * 128: c * S + (t + 1) * 128],
                                wob[:, c * D + nb * 512: c * D + (nb + 1) * 512],
                                start=(c == 0), stop=(c == DC - 1),
                                skip_group_check=True)
                    ins.then_inc(self.s_womm.h, 1)
                self.add("tensor", _womm)
                v_w = self.s_womm.inc()
                self.add("vector", lambda e, t=t, sl=sl, ls=lsem, w=v_w,
                         wl=v_ldx: (
                    e.wait_ge(self.s_womm.h, w),
                    e.wait_ge(ls.h, wl),
                    e.tensor_add(x1[:, t * D:(t + 1) * D], psw[sl][:, :],
                                 xr[sl][:, :]).then_inc(self.s_x1.h, 1)))
                self.s_x1.inc()

    def p3(self, s, sc2, x1):
        S, NT = self.S, self.NT
        with ExitStack() as sc:
            zT = self.sb(sc, f"zT{s}", [128, DC * S], BF16)
            wdbuf = [self.sb(sc, f"wdb{s}_{i}", [128, DC * HE], BF16)
                     for i in range(2)]
            wupbuf = [self.sb(sc, f"wub{s}_{i}", [HE, D], BF16)
                      for i in range(2)]
            wdsel = self.sb(sc, f"wds{s}", [128, DC * HE], BF16)
            wupsel = self.sb(sc, f"wus{s}", [HE, D], BF16)
            bdsel = self.sb(sc, f"bds{s}", [HE, 1], F32)
            selbc = self.sb(sc, f"selbc{s}", [128, 16], F32)
            bg = self.sb(sc, f"bg{s}", [128, D], F32)
            hT = self.sb(sc, f"hT{s}", [64, S], BF16)
            u2b = [self.sb(sc, f"u2{s}_{i}", [128, D], F32) for i in range(2)]
            t3b = [self.sb(sc, f"t3{s}_{i}", [128, D], F32) for i in range(2)]
            zb = [self.sb(sc, f"zb{s}_{i}", [128, D], BF16) for i in range(2)]
            gz = self.sb(sc, f"gz{s}", [128, D], BF16)
            st6 = [self.sb(sc, f"st6b{s}_{i}", [128, 12], F32)
                   for i in range(2)]
            mvA = [self.sb(sc, f"mvA{s}_{i}", [128, 2], F32) for i in range(2)]
            mvB = [self.sb(sc, f"mvB{s}_{i}", [128, 2], F32) for i in range(2)]
            rsA = [self.sb(sc, f"rsA{s}_{i}", [128, 1], F32) for i in range(2)]
            rsB = [self.sb(sc, f"rsB{s}_{i}", [128, 1], F32) for i in range(2)]
            rows = self.sb(sc, f"rows{s}", [1, D + 64], F32)
            crow4 = self.sb(sc, f"crow4_{s}", [1, 4 * D], F32)
            x1_all = self.s_x1.n
            self.add("sync", lambda e, wp=x1_all: (
                e.wait_ge(self.s_x1.h, wp),
                e.dma_start(out=crow4[:, :], in_=self.din["crow"][:, 0:4 * D])
                .then_inc(self.s_ldw2.h, 16)))
            v_crow = self.s_ldw2.inc(16)
            ohgcol = self.sb(sc, f"ohg{s}", [NE, 1], F32)
            zbnT_t = self.sb(sc, f"zbnT{s}", [128, NE], F32)
            t2buf = [self.sb(sc, f"t2o{s}_{i}", [128, D], F32) for i in range(2)]
            x2buf = [self.sb(sc, f"x2o{s}_{i}", [128, D], F32) for i in range(2)]

            R_zs = rows[:, 0:D]
            R_zbn = R_zs
            R_bup = R_zs
            o3 = D
            R_lg = rows[:, o3:o3 + NE]
            R_prob = rows[:, o3 + NE:o3 + 2 * NE]
            R_oh = rows[:, o3 + 2 * NE:o3 + 3 * NE]
            R_ohg = rows[:, o3 + 3 * NE:o3 + 4 * NE]
            nmx = rows[:, o3 + 4 * NE:o3 + 4 * NE + 1]
            sume = rows[:, o3 + 4 * NE + 1:o3 + 4 * NE + 2]
            rsum = rows[:, o3 + 4 * NE + 2:o3 + 4 * NE + 3]
            gate = rows[:, o3 + 4 * NE + 3:o3 + 4 * NE + 4]


            with ExitStack() as sa:
                psz = [self.ps(sa, f"psz{s}_{i}", [1, 512], F32) for i in range(2)]
                psztr = [self.ps(sa, f"pzt{s}_{i}", [128, 1024], BF16)
                         for i in range(2)]
                b_zmm = self.s_zmm.n
                b_z = self.s_z.n
                b_ztr = self.s_ztr.n
                b_zT = self.s_zT.n
                b_st3 = self.s_stat.n
                b_rs3 = self.s_rstd.n
                b_t3 = self.s_t3.n

                V1, V2, VR1, VR2 = {}, {}, {}, {}

                def q_s1(t):
                    sl = t % 2
                    c0 = self.s_ch.inc()
                    wp = V2.get(t - 2, 0)
                    V1[t] = self.s_stat.inc()
                    self.add("vector", lambda e, t=t, sl=sl, c0=c0, wp=wp: (
                        e.wait_ge(self.s_stat.h, wp) if wp else None,
                        e.bn_stats(st6[sl][:, 0:6], x1[:, t * D:t * D + 512]),
                        e.bn_stats(st6[sl][:, 6:12], x1[:, t * D + 512:(t + 1) * D])
                         .then_inc(self.s_ch.h, 1),
                        e.wait_ge(self.s_ch.h, c0),
                        e.bn_aggr(mvA[sl][:, :], st6[sl][:, :])
                         .then_inc(self.s_stat.h, 1)))

                def q_r1(t):
                    sl = t % 2
                    c1 = self.s_cha.inc()
                    VR1[t] = self.s_rstd.inc()
                    self.add("scalar", lambda e, sl=sl, w=V1[t],
                             c1=c1: (
                        e.wait_ge(self.s_stat.h, w),
                        e.activation(rsA[sl][:, :], mvA[sl][:, 1:2], AF.Ln,
                                     bias=self.ecol[:, 0:1])
                         .then_inc(self.s_cha.h, 1),
                        e.wait_ge(self.s_cha.h, c1),
                        e.activation(rsA[sl][:, :], rsA[sl][:, :], AF.Exp,
                                     scale=-0.5).then_inc(self.s_rstd.h, 1)))

                def q_s2(t):
                    sl = t % 2
                    c1 = self.s_ch.inc()
                    c2 = self.s_ch.inc()
                    V2[t] = self.s_stat.inc()
                    self.add("vector", lambda e, t=t, sl=sl, w=VR1[t],
                             c1=c1, c2=c2: (
                        e.wait_ge(self.s_rstd.h, w),
                        e.tensor_scalar(u2b[sl][:, :], x1[:, t * D:(t + 1) * D],
                                        mvA[sl][:, 0:1], rsA[sl][:, 0:1],
                                        ALU.subtract, ALU.mult)
                         .then_inc(self.s_ch.h, 1),
                        e.wait_ge(self.s_ch.h, c1),
                        e.bn_stats(st6[sl][:, 0:6], u2b[sl][:, 0:512]),
                        e.bn_stats(st6[sl][:, 6:12], u2b[sl][:, 512:1024])
                         .then_inc(self.s_ch.h, 1),
                        e.wait_ge(self.s_ch.h, c2),
                        e.bn_aggr(mvB[sl][:, :], st6[sl][:, :])
                         .then_inc(self.s_stat.h, 1)))

                def q_r2(t):
                    sl = t % 2
                    c1 = self.s_cha.inc()
                    VR2[t] = self.s_rstd.inc()
                    self.add("scalar", lambda e, sl=sl, w=V2[t],
                             c1=c1: (
                        e.wait_ge(self.s_stat.h, w),
                        e.activation(rsB[sl][:, :], mvB[sl][:, 1:2], AF.Ln,
                                     bias=self.ecol[:, 0:1])
                         .then_inc(self.s_cha.h, 1),
                        e.wait_ge(self.s_cha.h, c1),
                        e.activation(rsB[sl][:, :], rsB[sl][:, :], AF.Exp,
                                     scale=-0.5).then_inc(self.s_rstd.h, 1)))

                def q_t3(t):
                    sl = t % 2
                    w_zmm = b_zmm + t - 1 if t >= 2 else 0
                    w_z = b_z + t - 1 if t >= 2 else 0
                    self.s_t3.inc()
                    self.add("vector", lambda e, sl=sl, w=VR2[t],
                             w2=w_zmm, w3=w_z: (
                        e.wait_ge(self.s_rstd.h, w),
                        e.wait_ge(self.s_zmm.h, w2) if w2 else None,
                        e.wait_ge(self.s_z.h, w3) if w3 else None,
                        e.tensor_scalar(t3b[sl][:, :], u2b[sl][:, :],
                                        mvB[sl][:, 0:1], rsB[sl][:, 0:1],
                                        ALU.subtract, ALU.mult)
                         .then_inc(self.s_t3.h, 1)))

                def q_zmm(t):
                    sl = t % 2

                    def _zmm(e, t=t, sl=sl, w=b_t3 + t + 1):
                        e.wait_ge(self.s_t3.h, w)
                        for nb in range(2):
                            ins = e.matmul(psz[nb][:, :], self.ones[:, 0:1],
                                           t3b[sl][:, nb * 512:(nb + 1) * 512],
                                           start=(t == 0), stop=(t == NT - 1),
                                           skip_group_check=True)
                        ins.then_inc(self.s_zmm.h, 1)
                    self.add("tensor", _zmm)
                    self.s_zmm.inc()

                def q_z(t):
                    sl = t % 2
                    w_ztr = b_ztr + t - 1 if t >= 2 else 0
                    c1 = self.s_chg.inc()
                    w_own = b_z + t if t >= 1 else 0
                    self.add("gpsimd", lambda e, sl=sl, w=b_t3 + t + 1,
                             w2=w_ztr, c1=c1, wo=w_own: (
                        e.wait_ge(self.s_t3.h, w),
                        e.wait_ge(self.s_ztr.h, w2) if w2 else None,
                        e.wait_ge(self.s_z.h, wo) if wo else None,
                        e.tensor_mul(gz[:, :], t3b[sl][:, :], self.gtile[:, :])
                         .then_inc(self.s_chg.h, 1),
                        e.wait_ge(self.s_chg.h, c1),
                        e.tensor_add(zb[sl][:, :], gz[:, :], self.btile[:, :])
                         .then_inc(self.s_z.h, 1)))
                    self.s_z.inc()

                def q_ztr(t):
                    sl = t % 2
                    w_zT = b_zT + t - 1 if t >= 2 else 0

                    def _ztr(e, sl=sl, w=b_z + t + 1, w2=w_zT):
                        e.wait_ge(self.s_z.h, w)
                        if w2:
                            e.wait_ge(self.s_zT.h, w2)
                        for c in range(DC):
                            ins = e.transpose(psztr[sl][:, c * 128:(c + 1) * 128],
                                              zb[sl][:, c * 128:(c + 1) * 128],
                                              self.ident[:, 0:128])
                        ins.then_inc(self.s_ztr.h, 1)
                    self.add("tensor", _ztr)
                    self.s_ztr.inc()
                    dstap = (zT[:, :].rearrange("p (c q) -> p c q", q=S)
                             [:, :, t * 128:(t + 1) * 128])
                    srcap = psztr[sl][:, :].rearrange("p (c f) -> p c f", f=128)
                    self.add("vector", lambda e, d=dstap, sr=srcap,
                             w=b_ztr + t + 1: (
                        e.wait_ge(self.s_ztr.h, w),
                        e.tensor_copy(d, sr).then_inc(self.s_zT.h, 1)))
                    self.s_zT.inc()

                for t in range(NT):
                    q_s1(t)
                    q_r1(t)
                    if t >= 1:
                        q_s2(t - 1)
                        q_r2(t - 1)
                    if t >= 2:
                        q_t3(t - 2)
                        q_zmm(t - 2)
                        q_z(t - 2)
                        q_ztr(t - 2)
                q_s2(NT - 1)
                q_r2(NT - 1)
                for t in (NT - 2, NT - 1):
                    q_t3(t)
                    q_zmm(t)
                    q_z(t)
                    q_ztr(t)
                zT_done = self.s_zT.n
                zmm_done = self.s_zmm.n
                self.add("scalar", lambda e, w=zmm_done: (
                    e.wait_ge(self.s_zmm.h, w),
                    e.copy(R_zs[:, 0:512], psz[0][:, :]),
                    e.copy(R_zs[:, 512:1024], psz[1][:, :])
                     .then_inc(self.s_gat.h, 1)))
                v_g = self.s_gat.inc()

            with ExitStack() as sg:
                pzb = self.ps(sg, f"pzb{s}", [128, NE], F32)
                plg = self.ps(sg, f"plg{s}", [1, NE], F32)
                pohbc = self.ps(sg, f"poh{s}", [128, 16], F32)
                pbup = [self.ps(sg, f"pbu{s}_{i}", [1, 512], F32) for i in range(2)]
                pbg = self.ps(sg, f"pbg{s}", [128, 1024], F32)
                pohg = self.ps(sg, f"pog{s}", [NE, 1], F32)

                cb = self.s_ch.n
                self.s_ch.inc(5)

                def _zbn(e, w=v_g, cb=cb, wc=v_crow):
                    e.wait_ge(self.s_gat.h, w)
                    e.wait_ge(self.s_ldw2.h, wc)
                    e.tensor_scalar_mul(R_zs, R_zs, 1.0 / S).then_inc(self.s_ch.h, 1)
                    e.wait_ge(self.s_ch.h, cb + 1)
                    e.tensor_mul(R_zs, R_zs, crow4[0:1, 2 * D:3 * D]) \
                     .then_inc(self.s_ch.h, 1)
                    e.wait_ge(self.s_ch.h, cb + 2)
                    e.tensor_add(R_zs, R_zs, crow4[0:1, 3 * D:4 * D]) \
                     .then_inc(self.s_ch.h, 1)
                    e.wait_ge(self.s_ch.h, cb + 3)
                    e.tensor_sub(R_zbn, R_zs, crow4[0:1, 0:D]) \
                     .then_inc(self.s_ch.h, 1)
                    e.wait_ge(self.s_ch.h, cb + 4)
                    e.tensor_mul(R_zbn, R_zbn, self.alpha[0:1, :]) \
                     .then_inc(self.s_ch.h, 1)
                    e.wait_ge(self.s_ch.h, cb + 5)
                    e.tensor_add(R_zbn, R_zbn, crow4[0:1, D:2 * D]) \
                     .then_inc(self.s_gat.h, 1)
                self.add("vector", _zbn)
                v_g = self.s_gat.inc()

                def _zbnT(e, w=v_g):
                    e.wait_ge(self.s_gat.h, w)
                    for c in range(DC):
                        ins = e.transpose(pzb[:, c:c + 1],
                                          R_zbn[:, c * 128:(c + 1) * 128],
                                          self.ones[0:1, 0:1])
                    ins.then_inc(self.s_gat.h, 1)
                self.add("tensor", _zbnT)
                v_g = self.s_gat.inc()
                self.add("scalar", lambda e, w=v_g: (
                    e.wait_ge(self.s_gat.h, w),
                    e.copy(zbnT_t[:, :], pzb[:, :]).then_inc(self.s_gat.h, 1)))
                v_g = self.s_gat.inc()

                def _lg(e, w=v_g):
                    e.wait_ge(self.s_gat.h, w)
                    for c in range(DC):
                        ins = e.matmul(plg[:, :], zbnT_t[:, c:c + 1],
                                       self.wrsb[:, c * NE:(c + 1) * NE],
                                       start=(c == 0), stop=(c == DC - 1))
                    ins.then_inc(self.s_gat.h, 1)
                self.add("tensor", _lg)
                v_g = self.s_gat.inc()
                self.add("scalar", lambda e, w=v_g: (
                    e.wait_ge(self.s_gat.h, w),
                    e.copy(R_lg, plg[:, :]).then_inc(self.s_gat.h, 1)))
                v_g = self.s_gat.inc()

                def _sm1(e, w=v_g, c1=self.s_ch.inc()):
                    e.wait_ge(self.s_gat.h, w)
                    e.tensor_add(R_lg, R_lg, self.brsb[:, :]) \
                     .then_inc(self.s_ch.h, 1)
                    e.wait_ge(self.s_ch.h, c1)
                    e.reduce_max(nmx, R_lg, axis=AX.X, negate=True) \
                     .then_inc(self.s_gat.h, 1)
                self.add("vector", _sm1)
                v_g = self.s_gat.inc()
                self.add("scalar", lambda e, w=v_g: (
                    e.wait_ge(self.s_gat.h, w),
                    e.activation(R_prob, R_lg, AF.Exp, bias=nmx,
                                 accum_out=sume).then_inc(self.s_gat.h, 1)))
                v_g = self.s_gat.inc()

                def _sm2(e, w=v_g):
                    e.wait_ge(self.s_gat.h, w)
                    e.reciprocal(rsum, sume).then_inc(self.s_gat.h, 1)
                self.add("vector", _sm2)
                v_g = self.s_gat.inc()

                def _sm3(e, w=v_g):
                    e.wait_ge(self.s_gat.h, w)
                    e.tensor_single_scalar(R_prob, R_prob, rsum, ALU.mult) \
                     .then_inc(self.s_gat.h, 1)
                self.add("vector", _sm3)
                v_g = self.s_gat.inc()

                def _sm4(e, w=v_g):
                    e.wait_ge(self.s_gat.h, w)
                    e.reduce_max(gate, R_prob, axis=AX.X).then_inc(self.s_gat.h, 1)
                self.add("vector", _sm4)
                v_g = self.s_gat.inc()

                def _sm5(e, w=v_g):
                    e.wait_ge(self.s_gat.h, w)
                    e.tensor_single_scalar(R_oh, R_prob, gate, ALU.is_equal) \
                     .then_inc(self.s_gat.h, 1)
                self.add("vector", _sm5)
                v_g = self.s_gat.inc()

                def _sm6(e, w=v_g):
                    e.wait_ge(self.s_gat.h, w)
                    e.tensor_single_scalar(R_ohg, R_oh, gate, ALU.mult) \
                     .then_inc(self.s_gat.h, 1)
                self.add("vector", _sm6)
                v_g = self.s_gat.inc()

                def _bc(e, w=v_g):
                    e.wait_ge(self.s_gat.h, w)
                    e.matmul(pohbc[:, 0:NE], self.ones[0:1, :], R_oh,
                             start=True, stop=True)
                    e.matmul(pohbc[:, NE:2 * NE], self.ones[0:1, :], R_ohg,
                             start=True, stop=True, skip_group_check=True)
                    e.transpose(pohg[:, :], R_ohg, self.ones[0:1, 0:1]) \
                     .then_inc(self.s_gat.h, 1)
                self.add("tensor", _bc)
                v_g = self.s_gat.inc()
                self.add("scalar", lambda e, w=v_g: (
                    e.wait_ge(self.s_gat.h, w),
                    e.copy(selbc[:, :], pohbc[:, :]),
                    e.copy(ohgcol[:, :], pohg[:, :]).then_inc(self.s_gat.h, 1)))
                v_g = self.s_gat.inc()

                def _bup(e, w=v_g):
                    e.wait_ge(self.s_gat.h, w)
                    e.matmul(pbup[0][:, :], ohgcol[:, :],
                             self.bupall[:, 0:512], start=True, stop=True)
                    e.matmul(pbup[1][:, :], ohgcol[:, :],
                             self.bupall[:, 512:1024], start=True,
                             stop=True, skip_group_check=True) \
                     .then_inc(self.s_gat.h, 1)
                self.add("tensor", _bup)
                v_g = self.s_gat.inc()
                self.add("scalar", lambda e, w=v_g: (
                    e.wait_ge(self.s_gat.h, w),
                    e.copy(R_bup[:, 0:512], pbup[0][:, :]),
                    e.copy(R_bup[:, 512:1024], pbup[1][:, :])
                     .then_inc(self.s_gat.h, 1)))
                v_g = self.s_gat.inc()

                def _bg(e, w=v_g):
                    e.wait_ge(self.s_gat.h, w)
                    for nb in range(2):
                        ins = e.matmul(pbg[:, nb * 512:(nb + 1) * 512],
                                       self.ones[0:1, :],
                                       R_bup[:, nb * 512:(nb + 1) * 512],
                                       start=True, stop=True,
                                       skip_group_check=True)
                    ins.then_inc(self.s_gat.h, 1)
                self.add("tensor", _bg)
                v_g = self.s_gat.inc()
                self.add("vector", lambda e, w=v_g: (
                    e.wait_ge(self.s_gat.h, w),
                    e.tensor_copy(bg[:, :], pbg[:, :]).then_inc(self.s_gat.h, 1)))
                v_bg = self.s_gat.inc()

                b_sel = self.s_sel.n
                for ei in range(NE):
                    sl = ei % 2
                    esem = self.s_ldw3 if sl == 0 else self.s_ldw3b
                    w_gp = b_sel + ei - 1 if ei >= 2 else 0
                    w_self = esem.n if ei < 2 else 0
                    v_lw = esem.inc(32)
                    self.add("sync", lambda e, ei=ei, sl=sl, es=esem, w=w_gp,
                             wsf=w_self, wx=x1_all if ei < 2 else 0: (
                        e.wait_ge(self.s_sel.h, w) if w else None,
                        e.wait_ge(es.h, wsf) if wsf else None,
                        e.wait_ge(self.s_x1.h, wx) if wx else None,
                        e.dma_start(out=wdbuf[sl][:, :],
                                    in_=self.wd_b[:, ei * DC * HE:
                                                  (ei + 1) * DC * HE])
                         .then_inc(es.h, 16),
                        e.dma_start(out=wupbuf[sl][:, :],
                                    in_=self.wup_b[:, ei * D:(ei + 1) * D])
                         .then_inc(es.h, 16)))

                    def _sel(e, ei=ei, sl=sl, es=esem, w=v_lw, wg=v_bg,
                             cprev=self.s_sel.n):
                        if ei == 0:
                            e.wait_ge(self.s_gat.h, wg)
                        else:
                            e.wait_ge(self.s_sel.h, cprev)
                        e.wait_ge(es.h, w)
                        if ei == 0:
                            e.tensor_scalar(wdsel[:, :], wdbuf[sl][:, :],
                                            selbc[:, 0:1], None, ALU.mult)
                            e.tensor_scalar(bdsel[:, :], self.bdall[:, 0:1],
                                            selbc[0:HE, 0:1], None, ALU.mult)
                            ins = e.tensor_scalar(
                                wupsel[:, :], wupbuf[sl][:, :],
                                selbc[0:HE, NE:NE + 1], None, ALU.mult)
                        else:
                            e.scalar_tensor_tensor(
                                wdsel[:, :], wdbuf[sl][:, :],
                                selbc[:, ei:ei + 1], wdsel[:, :],
                                ALU.mult, ALU.add)
                            e.scalar_tensor_tensor(
                                bdsel[:, :], self.bdall[:, ei:ei + 1],
                                selbc[0:HE, ei:ei + 1], bdsel[:, :],
                                ALU.mult, ALU.add)
                            ins = e.scalar_tensor_tensor(
                                wupsel[:, :], wupbuf[sl][:, :],
                                selbc[0:HE, NE + ei:NE + ei + 1],
                                wupsel[:, :], ALU.mult, ALU.add)
                        ins.then_inc(self.s_sel.h, 1)
                    self.add("vector", _sel)
                    self.s_sel.inc()
                v_sel = self.s_sel.n

            with ExitStack() as sy:
                psh = [self.ps(sy, f"psh{s}_{i}", [HE, 512], F32)
                       for i in range(2)]
                psy = [self.ps(sy, f"psy{s}_{i}", [128, 1024], F32)
                       for i in range(2)]
                NB = S // 512
                b_hr = self.s_hrelu.n
                for tb in range(NB):
                    sl = tb % 2
                    w_hr = b_hr + tb - 1 if tb >= 2 else 0

                    def _hmm(e, tb=tb, sl=sl, w=v_sel, w2=w_hr, wz=zT_done,
                             first=(tb == 0)):
                        if first:
                            e.wait_ge(self.s_sel.h, w)
                            e.wait_ge(self.s_zT.h, wz)
                        if w2:
                            e.wait_ge(self.s_hrelu.h, w2)
                        for c in range(DC):
                            ins = e.matmul(
                                psh[sl][:, :],
                                wdsel[:, c * HE:(c + 1) * HE],
                                zT[:, c * S + tb * 512: c * S + tb * 512 + 512],
                                start=(c == 0), stop=(c == DC - 1))
                        ins.then_inc(self.s_ymm.h, 1)
                    self.add("tensor", _hmm)
                    v_h = self.s_ymm.inc()
                    self.add("scalar", lambda e, tb=tb, sl=sl, w=v_h, ws=v_sel: (
                        e.wait_ge(self.s_ymm.h, w),
                        e.wait_ge(self.s_sel.h, ws),
                        e.activation(hT[:, tb * 512:(tb + 1) * 512],
                                     psh[sl][:, :], AF.Relu,
                                     bias=bdsel[:, 0:1])
                         .then_inc(self.s_hrelu.h, 1)))
                    self.s_hrelu.inc()
                b_t2 = self.s_t2.n
                b_x2 = self.s_x2.n
                for t in range(NT):
                    sl = t % 2
                    w_hr = b_hr + t // 4 + 1
                    w_t2 = b_t2 + t - 1 if t >= 2 else 0

                    def _ymm(e, t=t, sl=sl, w=w_hr, w2=w_t2):
                        e.wait_ge(self.s_hrelu.h, w)
                        if w2:
                            e.wait_ge(self.s_t2.h, w2)
                        for nb in range(2):
                            ins = e.matmul(
                                psy[sl][:, nb * 512:(nb + 1) * 512],
                                hT[:, t * 128:(t + 1) * 128],
                                wupsel[:, nb * 512:(nb + 1) * 512],
                                start=True, stop=True, skip_group_check=True)
                        ins.then_inc(self.s_ymm.h, 1)
                    self.add("tensor", _ymm)
                    v_y = self.s_ymm.inc()
                    w_x2 = b_x2 + t - 1 if t >= 2 else 0
                    self.add("vector", lambda e, sl=sl, w=v_y, wb=v_bg, w2=w_x2: (
                        e.wait_ge(self.s_ymm.h, w),
                        e.wait_ge(self.s_gat.h, wb),
                        e.wait_ge(self.s_x2.h, w2) if w2 else None,
                        e.tensor_add(t2buf[sl][:, :], psy[sl][:, :], bg[:, :])
                         .then_inc(self.s_t2.h, 1)))
                    v_t2 = self.s_t2.inc()
                    ssem = self.s_st if t % 2 == 0 else self.s_st2
                    w_st = ssem.n if t >= 2 else 0
                    self.add("gpsimd", lambda e, t=t, sl=sl, ss=ssem, w=v_t2,
                             w2=w_st: (
                        e.wait_ge(self.s_t2.h, w),
                        e.wait_ge(ss.h, w2) if w2 else None,
                        e.tensor_add(x2buf[sl][:, :], t2buf[sl][:, :],
                                     x1[:, t * D:(t + 1) * D])
                         .then_inc(self.s_x2.h, 1)))
                    v_x2 = self.s_x2.inc()
                    self.add("sync", lambda e, s_=s, t=t, sl=sl, ss=ssem, w=v_x2: (
                        e.wait_ge(self.s_x2.h, w),
                        e.dma_start(out=self.out_hbm[s_, t * 128:(t + 1) * 128, :],
                                    in_=x2buf[sl][:, :])
                         .then_inc(ss.h, 16)))
                    ssem.inc(16)


def _host_inputs(xs, W_qkv, W_o, ln_g, ln_b, bn_g, bn_b, bn_mean, bn_var,
                 Wr, br, W_down, b_down, W_up, b_up, S):
    d = {}
    d["x"] = np.ascontiguousarray(xs, dtype=np.float32)

    wqk = np.empty((NH, DC, 128, 128), np.float32)
    for i in range(NH // 2):
        qrows = np.concatenate([W_qkv[(2 * i) * 64 + PERM64],
                                W_qkv[(2 * i + 1) * 64 + PERM64]], axis=0)
        krows = np.concatenate([W_qkv[D + (2 * i) * 64 + PERM64],
                                W_qkv[D + (2 * i + 1) * 64 + PERM64]], axis=0)
        for c in range(DC):
            wqk[2 * i, c] = qrows[:, c * 128:(c + 1) * 128].T
            wqk[2 * i + 1, c] = krows[:, c * 128:(c + 1) * 128].T
    d["wqk"] = wqk

    wv = np.empty((DC, 128, D), np.float32)
    wo = np.empty((DC, 128, D), np.float32)
    for c in range(DC):
        wv[c] = W_qkv[2 * D:3 * D, c * 128:(c + 1) * 128].T
        wo[c] = W_o[:, c * 128:(c + 1) * 128].T
    d["wv"] = wv
    d["wo"] = wo

    wd = np.transpose(W_down.reshape(NE, DC, 128, HE), (2, 0, 1, 3))
    d["wd"] = np.ascontiguousarray(wd.reshape(128, NE * DC * HE), np.float32)
    d["wup"] = np.ascontiguousarray(
        np.transpose(W_up, (1, 0, 2)).reshape(HE, NE * D), np.float32)

    pos = np.arange(S, dtype=np.float64)
    inv = THETA ** (-np.arange(0, DK, 2, dtype=np.float64) / DK)
    A64 = np.empty((64, S), np.float64)
    B64 = np.empty((64, S), np.float64)
    for blk in range(2):
        pair = np.arange(16 * blk, 16 * blk + 16)
        ang = pos[None, :] * inv[pair][:, None]
        A64[blk * 32:blk * 32 + 16] = np.cos(ang)
        A64[blk * 32 + 16:blk * 32 + 32] = np.cos(ang)
        B64[blk * 32:blk * 32 + 16] = -np.sin(ang)
        B64[blk * 32 + 16:blk * 32 + 32] = np.sin(ang)
    d["ropeA"] = np.tile(A64, (2, 1)).astype(np.float32)
    d["ropeB"] = np.tile(B64, (2, 1)).astype(np.float32)

    d["gtile"] = np.tile(ln_g[None, :], (128, 1)).astype(np.float32)
    d["btile"] = np.tile(ln_b[None, :], (128, 1)).astype(np.float32)
    import ml_dtypes
    d["ident"] = np.concatenate([np.eye(128), np.zeros((128, 512))],
                                axis=1).astype(ml_dtypes.bfloat16)
    d["ones"] = np.ones((128, 128), np.float32)
    d["crow"] = np.concatenate(
        [bn_mean, bn_b, ln_g, ln_b, bn_var, bn_g]).reshape(1, 6 * D).astype(np.float32)
    d["wr"] = np.ascontiguousarray(
        np.transpose(Wr.reshape(DC, 128, NE), (1, 0, 2)).reshape(128, DC * NE),
        np.float32)
    d["br"] = br.reshape(1, NE).astype(np.float32)
    d["bdall"] = np.ascontiguousarray(b_down.T, np.float32)
    d["bupall"] = np.ascontiguousarray(b_up, np.float32)
    d["ecol"] = np.tile(np.array([[EPS, -CSHIFT]], np.float32), (128, 1))
    return d


def kernel(L0_attn_x, L0_attn_W_qkv, L0_attn_W_o, L0_ffn_ln_g, L0_ffn_ln_b,
           L0_ffn_bn_g, L0_ffn_bn_b, L0_ffn_bn_mean, L0_ffn_bn_var,
           L0_ffn_Wr, L0_ffn_br, L0_ffn_W_down, L0_ffn_b_down, L0_ffn_W_up,
           L0_ffn_b_up):
    Bfull, S, _ = L0_attn_x.shape
    B_core = Bfull // NCORES
    b = Builder(B_core, S)
    nc = b.build()
    in_maps = []
    for core in range(NCORES):
        xs = L0_attn_x[core * B_core:(core + 1) * B_core]
        in_maps.append(_host_inputs(
            xs, L0_attn_W_qkv, L0_attn_W_o, L0_ffn_ln_g, L0_ffn_ln_b,
            L0_ffn_bn_g, L0_ffn_bn_b, L0_ffn_bn_mean, L0_ffn_bn_var,
            L0_ffn_Wr, L0_ffn_br, L0_ffn_W_down, L0_ffn_b_down, L0_ffn_W_up,
            L0_ffn_b_up, S))
    res = run_bass_kernel_spmd(nc, in_maps, core_ids=list(range(NCORES)))
    out = np.concatenate([r["out"] for r in res.results], axis=0)
    return out.astype(np.float32)


# version 12
# speedup vs baseline: 1.0165x; 1.0165x over previous
from collections import defaultdict
from contextlib import ExitStack

import numpy as np

import concourse.bass as bass
import concourse.mybir as mybir
from concourse.bass_utils import run_bass_kernel_spmd

F32 = mybir.dt.float32
BF16 = mybir.dt.bfloat16
AF = mybir.ActivationFunctionType
ALU = mybir.AluOpType
AX = mybir.AxisListType

D = 1024
NH = 16
DK = 64
NE = 8
HE = 64
THETA = 10000.0
EPS = 1e-5
CSHIFT = 24.0
NCORES = 8
DC = D // 128

PERM64 = []
for _blk in range(2):
    PERM64 += [2 * (16 * _blk + j) for j in range(16)]
    PERM64 += [2 * (16 * _blk + j) + 1 for j in range(16)]
PERM64 = np.array(PERM64)
SHUF16 = list(range(16, 32)) + list(range(16))


class Sem:
    def __init__(self, stack, nc, name):
        self.h = stack.enter_context(nc.semaphore(name))
        self.n = 0

    def inc(self, amt=1):
        self.n += amt
        return self.n


class Builder:
    ENGS = ("sync", "tensor", "scalar", "vector", "gpsimd")

    def __init__(self, B_core, S):
        assert S % 512 == 0
        self.BC, self.S = B_core, S
        self.NT = S // 128
        self.NQB = S // 512
        self.nc = bass.Bass("TRN2", target_bir_lowering=False)
        self.plan = defaultdict(list)
        self.st = ExitStack()
        self.gpair = 0
        self.ghqb = 0
        self.pt_free = defaultdict(int)
        self.pss_free = defaultdict(int)

    def add(self, eng, fn):
        self.plan[eng].append(fn)

    def flush(self):
        plan, self.plan = self.plan, defaultdict(list)
        if not any(plan.values()):
            return
        with self.nc.Block() as blk:
            for eng in self.ENGS:
                if plan[eng]:
                    getattr(blk, eng)(
                        lambda e, fns=tuple(plan[eng]): [f(e) for f in fns])

    def warm(self, psum_ap, n):
        def _w(e, n=n, p=psum_ap):
            for _ in range(n):
                e.matmul(p, self.ident[:, 0:128], self.ident[:, 128:640],
                         start=True, stop=True, skip_group_check=True)
        self.add("tensor", _w)

    def sb(self, stack, name, shape, dtype, side="left"):
        return stack.enter_context(
            self.nc.sbuf_tensor(name, shape, dtype, side=side))

    def ps(self, stack, name, shape, dtype):
        return stack.enter_context(self.nc.psum_tensor(name, shape, dtype))

    def build(self):
        nc, st = self.nc, self.st
        S, BC = self.S, self.BC

        self.x_hbm = nc.dram_tensor("x", [BC, S, D], F32, kind="ExternalInput").ap()
        self.out_hbm = nc.dram_tensor("out", [BC, S, D], F32,
                                      kind="ExternalOutput").ap()
        dram_in = [
            ("wqk", [NH, DC, 128, 128]), ("wv", [DC, 128, D]), ("wo", [DC, 128, D]),
            ("wd", [128, NE * DC * HE]), ("wup", [HE, NE * D]),
            ("ropeA", [128, S]), ("ropeB", [128, S]),
            ("gtile", [128, D]), ("btile", [128, D]), ("ones", [128, 128]),
            ("crow", [1, 6 * D]),
            ("wr", [128, DC * NE]), ("br", [1, NE]),
            ("bdall", [HE, NE]), ("bupall", [NE, D]), ("ecol", [128, 2]),
        ]
        self.din = {n: nc.dram_tensor(n, sh, F32, kind="ExternalInput").ap()
                    for n, sh in dram_in}
        self.din["ident"] = nc.dram_tensor("ident", [128, 640], BF16,
                                           kind="ExternalInput").ap()
        self.wqk_b = nc.dram_tensor("wqk_b", [NH, DC, 128, 128], BF16).ap()
        self.wv_b = nc.dram_tensor("wv_b", [DC, 128, D], BF16).ap()
        self.wo_b = nc.dram_tensor("wo_b", [DC, 128, D], BF16).ap()
        self.wd_b = nc.dram_tensor("wd_b", [128, NE * DC * HE], BF16).ap()
        self.wup_b = nc.dram_tensor("wup_b", [HE, NE * D], BF16).ap()

        with st:
            for name in ("c0 w0 ldx ldw ldw2 stat rstd u1 tr u1T qk rope qkT "
                         "vmm vev smm exp msk pvmm lrow bc norm womm x1 st ch w0b st2 "
                         "ldxb ldwb ldw3 ldw3b cha chg "
                         "t3 zmm z ztr zT gat sel hrelu ymm t2 x2 misc macc"
                         ).split():
                setattr(self, "s_" + name, Sem(st, nc, name))

            self.ropeA = self.sb(st, "ropeA_s", [128, S], F32)
            self.ropeB = self.sb(st, "ropeB_s", [128, S], F32)
            self.gtile = self.sb(st, "gtile_s", [128, D], F32)
            self.btile = self.sb(st, "btile_s", [128, D], F32)
            self.ident = self.sb(st, "ident_s", [128, 640], BF16)
            self.ones = self.sb(st, "ones_s", [128, 128], F32)
            self.wrsb = self.sb(st, "wr_s", [128, DC * NE], F32)
            self.brsb = self.sb(st, "br_s", [1, NE], F32)
            self.bdall = self.sb(st, "bd_s", [HE, NE], F32)
            self.bupall = self.sb(st, "bup_s", [NE, D], F32)
            self.alpha = self.sb(st, "alpha_s", [1, D], F32)
            self.ecol = self.sb(st, "ecol_s", [128, 2], F32)

            self.phase0()
            for s in range(BC):
                self.seq(s)

            self.add("sync", lambda e, w=self.s_st.n, w2=self.s_st2.n: (
                e.wait_ge(self.s_st.h, w), e.wait_ge(self.s_st2.h, w2)))
            self.flush()
        return nc

    def phase0(self):
        consts = [
            (self.ropeA, "ropeA"), (self.ropeB, "ropeB"),
            (self.gtile, "gtile"), (self.btile, "btile"),
            (self.ident, "ident"), (self.ones, "ones"),
            (self.wrsb, "wr"), (self.brsb, "br"),
            (self.bdall, "bdall"), (self.bupall, "bupall"),
            (self.ecol, "ecol"),
        ]
        for dst, name in consts:
            src = self.din[name]
            self.add("sync", lambda e, a=dst, b=src:
                     e.dma_start(out=a[:, :], in_=b[:, :]).then_inc(self.s_c0.h, 16))
            self.s_c0.inc(16)


        casts = []
        for h in range(NH):
            casts.append((self.din["wqk"][h].rearrange("c d e -> d c e"),
                          self.wqk_b[h].rearrange("c d e -> d c e"),
                          128, DC * 128))
        for c in range(DC):
            casts.append((self.din["wv"][c], self.wv_b[c], 128, D))
            casts.append((self.din["wo"][c], self.wo_b[c], 128, D))
        for q in range(4):
            n = NE * DC * HE // 4
            casts.append((self.din["wd"][:, q * n:(q + 1) * n],
                          self.wd_b[:, q * n:(q + 1) * n], 128, n))
        for q in range(4):
            n = NE * D // 4
            casts.append((self.din["wup"][:, q * n:(q + 1) * n],
                          self.wup_b[:, q * n:(q + 1) * n], HE, n))

        with ExitStack() as sc:
            cbuf_f = self.sb(sc, "cbuf_f", [128, 4096], F32)
            cbuf_b = self.sb(sc, "cbuf_b", [128, 4096], BF16)
            cvg = self.sb(sc, "cvg", [1, 2 * D], F32)
            self.add("sync", lambda e: e.dma_start(
                out=cvg[:, :], in_=self.din["crow"][:, 4 * D:6 * D])
                .then_inc(self.s_c0.h, 16))
            cw = self.s_c0.inc(16)

            c1 = self.s_cha.inc()

            def _alpha_act(e, w=cw, c1=c1):
                e.wait_ge(self.s_c0.h, w)
                e.activation(self.alpha[0:1, :], cvg[0:1, 0:D], AF.Ln,
                             bias=self.ecol[0:1, 0:1]).then_inc(self.s_cha.h, 1)
                e.wait_ge(self.s_cha.h, c1)
                e.activation(self.alpha[0:1, :], self.alpha[0:1, :], AF.Exp,
                             scale=-0.5).then_inc(self.s_misc.h, 1)
            self.add("scalar", _alpha_act)
            wm = self.s_misc.inc()

            def _alpha_dve(e, w=wm):
                e.wait_ge(self.s_misc.h, w)
                e.tensor_mul(self.alpha[0:1, :], self.alpha[0:1, :],
                             cvg[0:1, D:2 * D]).then_inc(self.s_misc.h, 1)
            self.add("vector", _alpha_dve)
            self.s_misc.inc()
            wsem = [self.s_w0, self.s_w0b]
            for i, (src, dst, npart, nf) in enumerate(casts):
                assert nf <= 2048
                off = (i % 2) * 2048
                bf = cbuf_f[0:npart, off:off + nf]
                bb = cbuf_b[0:npart, off:off + nf]
                if len(src.shape) == 3:
                    bf = bf.rearrange("p (c e) -> p c e", e=src.shape[2])
                    bb = bb.rearrange("p (c e) -> p c e", e=src.shape[2])
                ws = wsem[i % 2]
                w_st = ws.n if i >= 2 else 0
                w_ld = self.s_ldw.inc(16)
                self.add("sync", lambda e, s=src, bf=bf, ws=ws, w=w_st: (
                    e.wait_ge(ws.h, w) if w else None,
                    e.dma_start(out=bf, in_=s).then_inc(self.s_ldw.h, 16)))
                v_cast = self.s_macc.inc()
                self.add("vector", lambda e, bf=bf, bb=bb, ws=ws, w=w_ld, w2=w_st: (
                    e.wait_ge(self.s_ldw.h, w),
                    e.wait_ge(ws.h, w2) if w2 else None,
                    e.tensor_copy(bb, bf).then_inc(self.s_macc.h, 1)))
                self.add("sync", lambda e, d=dst, bb=bb, ws=ws, w=v_cast: (
                    e.wait_ge(self.s_macc.h, w),
                    e.dma_start(out=d, in_=bb).then_inc(ws.h, 16)))
                ws.inc(16)
            self.add("sync", lambda e, w=self.s_w0.n, w2=self.s_w0b.n: (
                e.wait_ge(self.s_w0.h, w), e.wait_ge(self.s_w0b.h, w2)))
            self.add("gpsimd", lambda e, w=self.s_c0.n: e.wait_ge(self.s_c0.h, w))
            self.add("vector", lambda e, w=self.s_misc.n: e.wait_ge(self.s_misc.h, w))
            self.flush()

    def seq(self, s):
        S, NT = self.S, self.NT
        scA = ExitStack()
        scQ = ExitStack()
        try:
            self.psq = [self.ps(scQ, f"psq{s}_{i}", [128, 512], F32)
                        for i in range(2)]
            with ExitStack() as sc1:
                qkT = self.sb(sc1, f"qkT{s}", [128, NH * S], BF16)
                vsb = self.sb(sc1, f"v{s}", [128, NT * NH * 66], BF16)
                self.p1(s, sc1, qkT, vsb)
                scQ.close()
                attnT = self.sb(scA, f"attnT{s}", [128, DC * S], BF16,
                                side="right")
                self.p2a(s, sc1, qkT, vsb, attnT)
            with ExitStack() as sc2:
                x1 = self.sb(sc2, f"x1_{s}", [128, NT * D], F32)
                self.p2b(s, sc2, attnT, x1)
                scA.close()
                self.p3(s, sc2, x1)
            self.flush()
        finally:
            scA.close()
            scQ.close()

    def p1(self, s, sc1, qkT, vsb):
        S, NT, NQB = self.S, self.NT, self.NQB
        with ExitStack() as sc:
            u1T = self.sb(sc, f"u1T_{s}", [128, DC * S], BF16)
            wvb = self.sb(sc, f"wvb{s}", [128, DC * D], BF16)

            v_vset = self.s_misc.inc()
            self.add("gpsimd", lambda e: e.memset(vsb[:, :], 1.0)
                     .then_inc(self.s_misc.h, 1))
            for c in range(DC):
                self.add("sync", lambda e, c=c: e.dma_start(
                    out=wvb[:, c * D:(c + 1) * D], in_=self.wv_b[c])
                    .then_inc(self.s_ldw2.h, 16))
                self.s_ldw2.inc(16)
            v_wv = self.s_ldw2.n

            with ExitStack() as sca:
                xbuf = [self.sb(sca, f"xb{s}_{i}", [128, D], F32) for i in range(2)]
                u1buf = [self.sb(sca, f"u1b{s}_{i}", [128, D], BF16) for i in range(2)]
                stat6 = [self.sb(sca, f"st6_{s}_{i}", [128, 12], F32)
                         for i in range(2)]
                mv = [self.sb(sca, f"mv_{s}_{i}", [128, 2], F32) for i in range(2)]
                rstd = [self.sb(sca, f"rs_{s}_{i}", [128, 1], F32)
                        for i in range(2)]
                ptr = [self.ps(sca, f"ptr{s}_0", [128, 1024], BF16)] * 2
                psv = [self.ps(sca, f"psv{s}_0", [128, 1024], F32)] * 2
                b_u1 = self.s_u1.n
                b_tr = self.s_tr.n
                b_ev = self.s_u1T.n
                b_stat = self.s_stat.n
                b_rstd = self.s_rstd.n
                b_vev = self.s_vev.n

                self.warm(psv[0][:, 0:512], 12)

                def st_load(t):
                    sl = t % 2
                    lsem = self.s_ldx if sl == 0 else self.s_ldxb
                    w_u1 = b_u1 + t - 1 if t >= 2 else 0
                    w_self = lsem.n if t < 2 else 0
                    wst = (self.s_st.n, self.s_st2.n) if t < 2 else (0, 0)
                    lsem.inc(16)
                    self.add("sync", lambda e, s_=s, t=t, sl=sl, ls=lsem,
                             w=w_u1, wsf=w_self, wst=wst: (
                        e.wait_ge(self.s_u1.h, w) if w else None,
                        e.wait_ge(ls.h, wsf) if wsf else None,
                        e.wait_ge(self.s_st.h, wst[0]) if wst[0] else None,
                        e.wait_ge(self.s_st2.h, wst[1]) if wst[1] else None,
                        e.dma_start(out=xbuf[sl][:, :],
                                    in_=self.x_hbm[s_, t * 128:(t + 1) * 128, :])
                         .then_inc(ls.h, 16)))

                def st_stats(t):
                    sl = t % 2
                    lsem = self.s_ldx if sl == 0 else self.s_ldxb
                    w_ld = lsem.n
                    c1 = self.s_ch.inc()
                    self.s_stat.inc()
                    self.add("vector", lambda e, sl=sl, ls=lsem, w=w_ld, c1=c1: (
                        e.wait_ge(ls.h, w),
                        e.bn_stats(stat6[sl][:, 0:6], xbuf[sl][:, 0:512]),
                        e.bn_stats(stat6[sl][:, 6:12], xbuf[sl][:, 512:1024])
                         .then_inc(self.s_ch.h, 1),
                        e.wait_ge(self.s_ch.h, c1),
                        e.bn_aggr(mv[sl][:, :], stat6[sl][:, :])
                         .then_inc(self.s_stat.h, 1)))

                def st_rstd(t):
                    sl = t % 2
                    c1 = self.s_cha.inc()
                    self.s_rstd.inc()
                    self.add("scalar", lambda e, sl=sl, w=b_stat + t + 1, c1=c1: (
                        e.wait_ge(self.s_stat.h, w),
                        e.activation(rstd[sl][:, :], mv[sl][:, 1:2], AF.Ln,
                                     bias=self.ecol[:, 0:1])
                         .then_inc(self.s_cha.h, 1),
                        e.wait_ge(self.s_cha.h, c1),
                        e.activation(rstd[sl][:, :], rstd[sl][:, :], AF.Exp,
                                     scale=-0.5).then_inc(self.s_rstd.h, 1)))

                def st_apply(t):
                    sl = t % 2
                    w_tr = b_tr + t - 1 if t >= 2 else 0
                    self.s_u1.inc()
                    self.add("vector", lambda e, sl=sl, w=b_rstd + t + 1,
                             w2=w_tr: (
                        e.wait_ge(self.s_rstd.h, w),
                        e.wait_ge(self.s_tr.h, w2) if w2 else None,
                        e.tensor_scalar(u1buf[sl][:, :], xbuf[sl][:, :],
                                        mv[sl][:, 0:1], rstd[sl][:, 0:1],
                                        ALU.subtract, ALU.mult)
                         .then_inc(self.s_u1.h, 1)))

                def st_tr(t):
                    sl = t % 2
                    w_ev = b_ev + t if t >= 1 else 0
                    self.s_tr.inc()

                    def _tr(e, sl=sl, w=b_u1 + t + 1, w2=w_ev):
                        e.wait_ge(self.s_u1.h, w)
                        if w2:
                            e.wait_ge(self.s_u1T.h, w2)
                        for c in range(DC):
                            ins = e.transpose(ptr[sl][:, c * 128:(c + 1) * 128],
                                              u1buf[sl][:, c * 128:(c + 1) * 128],
                                              self.ident[:, 0:128])
                        ins.then_inc(self.s_tr.h, 1)
                    self.add("tensor", _tr)

                def st_evac(t):
                    sl = t % 2
                    self.s_u1T.inc()
                    dstap = (u1T[:, :].rearrange("p (c q) -> p c q", q=S)
                             [:, :, t * 128:(t + 1) * 128])
                    srcap = ptr[sl][:, :].rearrange("p (c f) -> p c f", f=128)
                    self.add("vector", lambda e, d=dstap, sr=srcap,
                             w=b_tr + t + 1: (
                        e.wait_ge(self.s_tr.h, w),
                        e.tensor_copy(d, sr).then_inc(self.s_u1T.h, 1)))

                def plan_v(t):
                    sl = t % 2
                    w_ev2 = b_vev + t if t >= 1 else 0
                    w_u1T = b_ev + t + 1

                    def _vmm(e, t=t, sl=sl, w=v_wv, w2=w_ev2, wu=w_u1T,
                             first=(t == 0)):
                        if first:
                            e.wait_ge(self.s_ldw2.h, w)
                        e.wait_ge(self.s_u1T.h, wu)
                        if w2:
                            e.wait_ge(self.s_vev.h, w2)
                        for c in range(DC):
                            for nb in range(2):
                                ins = e.matmul(
                                    psv[sl][:, nb * 512:(nb + 1) * 512],
                                    u1T[:, c * S + t * 128: c * S + (t + 1) * 128],
                                    wvb[:, c * D + nb * 512: c * D + (nb + 1) * 512],
                                    start=(c == 0), stop=(c == DC - 1),
                                    skip_group_check=True)
                        ins.then_inc(self.s_vmm.h, 1)
                    self.add("tensor", _vmm)
                    v_vmm = self.s_vmm.inc()
                    dstap = (vsb[:, t * NH * 66:(t + 1) * NH * 66]
                             .rearrange("p (h c) -> p h c", c=66)[:, :, 0:64])
                    srcap = psv[sl][:, :].rearrange("p (h c) -> p h c", c=64)
                    wv_first = v_vset if t == 0 else 0
                    self.add("vector", lambda e, d=dstap, sr=srcap, w=v_vmm,
                             w2=wv_first: (
                        e.wait_ge(self.s_misc.h, w2) if w2 else None,
                        e.wait_ge(self.s_vmm.h, w),
                        e.tensor_copy(d, sr).then_inc(self.s_vev.h, 1)))
                    self.s_vev.inc()

                for t in range(NT):
                    st_load(t)
                    st_stats(t)
                    st_rstd(t)
                    if t >= 1:
                        st_apply(t - 1)
                        st_tr(t - 1)
                        st_evac(t - 1)
                    if t >= 2:
                        plan_v(t - 2)
                st_apply(NT - 1)
                st_tr(NT - 1)
                st_evac(NT - 1)
                plan_v(NT - 2)
                plan_v(NT - 1)
                u1T_done = self.s_u1T.n

            with ExitStack() as scb:
                wqkb = [self.sb(scb, f"wqk{s}_{i}", [128, DC * 128], BF16)
                        for i in range(2)]
                qks = [self.sb(scb, f"qks{s}_{i}", [128, 512], F32)
                       for i in range(2)]
                t1b = [self.sb(scb, f"t1b{s}_{i}", [128, 512], F32)
                       for i in range(2)]
                t2b = self.sb(scb, f"t2b{s}", [128, 512], F32)
                psq = self.psq
                self.qkT_base = self.s_qkT.n
                self.vev_base = b_vev
                b_qk = self.s_qk.n
                b_rope = self.s_rope.n
                b_qkT = self.s_qkT.n
                self.add("tensor", lambda e, w=u1T_done: e.wait_ge(self.s_u1T.h, w))
                for h in range(NH):
                    wsem = self.s_ldw if h % 2 == 0 else self.s_ldwb
                    w_q = b_qk + (h - 1) * NQB if h >= 2 else 0
                    w_self = wsem.n if h < 2 else 0
                    v_ldw = wsem.inc(16)
                    self.add("sync", lambda e, h=h, ws=wsem, w=w_q, wsf=w_self,
                             wu=u1T_done if h < 2 else 0: (
                        e.wait_ge(self.s_qk.h, w) if w else None,
                        e.wait_ge(ws.h, wsf) if wsf else None,
                        e.wait_ge(self.s_u1T.h, wu) if wu else None,
                        e.dma_start(out=wqkb[h % 2][:, :]
                                    .rearrange("d (c e) -> d c e", e=128),
                                    in_=self.wqk_b[h].rearrange("c d e -> d c e"))
                         .then_inc(ws.h, 16)))
                    for qb in range(NQB):
                        i = h * NQB + qb
                        w_ps = b_rope + i - 1 if i >= 2 else 0

                        wqsem = self.s_ldw if h % 2 == 0 else self.s_ldwb

                        def _qkmm(e, h=h, qb=qb, sl=i % 2, ws=wqsem,
                                  wl=v_ldw, wp=w_ps):
                            e.wait_ge(ws.h, wl)
                            if wp:
                                e.wait_ge(self.s_rope.h, wp)
                            for c in range(DC):
                                ins = e.matmul(
                                    psq[sl][:, :],
                                    wqkb[h % 2][:, c * 128:(c + 1) * 128],
                                    u1T[:, c * S + qb * 512: c * S + qb * 512 + 512],
                                    start=(c == 0), stop=(c == DC - 1))
                            ins.then_inc(self.s_qk.h, 1)
                        self.add("tensor", _qkmm)
                        v_qk = self.s_qk.inc()
                        w_gp = b_qkT + i - 1 if i >= 2 else 0
                        self.add("vector", lambda e, sl=i % 2, qb=qb, w=v_qk, w2=w_gp: (
                            e.wait_ge(self.s_qk.h, w),
                            e.wait_ge(self.s_qkT.h, w2) if w2 else None,
                            e.stream_shuffle(qks[sl][:, :], psq[sl][:, :], SHUF16),
                            e.tensor_mul(t1b[sl][:, :], psq[sl][:, :],
                                         self.ropeA[:, qb * 512:qb * 512 + 512])
                             .then_inc(self.s_rope.h, 1)))
                        v_rope = self.s_rope.inc()

                        c1 = self.s_chg.inc()
                        w_own = b_qkT + i if i >= 1 else 0

                        def _gp(e, h=h, qb=qb, sl=i % 2, w=v_rope, c1=c1,
                                wo=w_own):
                            e.wait_ge(self.s_rope.h, w)
                            if wo:
                                e.wait_ge(self.s_qkT.h, wo)
                            e.tensor_mul(t2b[:, :], qks[sl][:, :],
                                         self.ropeB[:, qb * 512:qb * 512 + 512]) \
                             .then_inc(self.s_chg.h, 1)
                            e.wait_ge(self.s_chg.h, c1)
                            e.tensor_add(qkT[:, h * S + qb * 512:
                                             h * S + qb * 512 + 512],
                                         t1b[sl][:, :], t2b[:, :]) \
                             .then_inc(self.s_qkT.h, 1)
                        self.add("gpsimd", _gp)
                        self.s_qkT.inc()

    def p2a(self, s, sc1, qkT, vsb, attnT):
        S, NQB = self.S, self.NQB
        with ExitStack() as sc:
            PT = self.sb(sc, f"PT{s}", [128, 8 * 512], BF16)
            lrow = [self.sb(sc, f"lr{s}_{i}", [1, 512], F32) for i in range(2)]
            rb = [self.sb(sc, f"rb{s}_{i}", [64, 512], F32) for i in range(2)]
            pss6 = self.ps(sc, f"pss{s}", [128, 3072], F32)
            pss = [pss6[:, i * 1024:(i + 1) * 1024] for i in range(3)]
            posb = [self.ps(sc, f"po{s}_{i}", [128, 512], F32) for i in range(2)]
            pso = posb

            self.warm(pss6[:, 1024:1536], 16)
            pending = None

            def plan_pv(h, qb, i_hqb, K, exp_done, msk_done, js):
                for j in js:
                    sigma = max(0, j - 4 * qb) * 128
                    sl4 = j % 8
                    wmask = msk_done.get(j, 0)
                    w_exp = exp_done[j // 2]

                    wv = self.vev_base + j + 1

                    def _pv(e, h=h, qb=qb, j=j, K=K, sl4=sl4, sigma=sigma,
                            we=w_exp, wm=wmask, i_hqb=i_hqb, wv=wv):
                        e.wait_ge(self.s_vev.h, wv)
                        e.wait_ge(self.s_exp.h, we)
                        if wm:
                            e.wait_ge(self.s_msk.h, wm)
                        if j == 0 and i_hqb >= 2:
                            e.wait_ge(self.s_norm.h, i_hqb - 1)
                        e.matmul(pso[i_hqb % 2][0:65, sigma:512],
                                 vsb[:, j * NH * 66 + h * 66:
                                     j * NH * 66 + h * 66 + 65],
                                 PT[:, sl4 * 512 + sigma: sl4 * 512 + 512],
                                 start=(j == 0), stop=(j == K - 1),
                                 skip_group_check=True) \
                         .then_inc(self.s_pvmm.h, 1)
                    self.add("tensor", _pv)
                    self.s_pvmm.inc()
                    self.pt_free[j % 8] = self.s_pvmm.n

            def plan_tail(h, qb, i_hqb):
                w_pv = self.s_pvmm.n
                self.add("scalar", lambda e, sl=i_hqb % 2, w=w_pv: (
                    e.wait_ge(self.s_pvmm.h, w),
                    e.copy(lrow[sl][:, :], pso[sl][64:65, :])
                     .then_inc(self.s_lrow.h, 1)))
                v_l = self.s_lrow.inc()
                w_rec = self.s_norm.n - 1 if i_hqb >= 2 else 0
                self.add("tensor", lambda e, sl=i_hqb % 2, w=v_l, w2=w_rec: (
                    e.wait_ge(self.s_lrow.h, w),
                    e.wait_ge(self.s_norm.h, w2) if w2 else None,
                    e.matmul(posb[sl][64:128, :], self.ones[0:1, 0:64],
                             lrow[sl][:, :], start=True, stop=True,
                             skip_group_check=True)
                     .then_inc(self.s_bc.h, 1)))
                v_bc = self.s_bc.inc()
                dst = attnT[(h % 2) * 64:(h % 2) * 64 + 64,
                            (h // 2) * S + qb * 512:(h // 2) * S + qb * 512 + 512]
                c1 = self.s_ch.inc()
                self.add("vector", lambda e, sl=i_hqb % 2, d=dst, w=v_bc, c1=c1: (
                    e.wait_ge(self.s_bc.h, w),
                    e.reciprocal(rb[sl][:, :], posb[sl][64:128, :])
                     .then_inc(self.s_ch.h, 1),
                    e.wait_ge(self.s_ch.h, c1),
                    e.tensor_mul(d, pso[sl][0:64, :], rb[sl][:, :])
                     .then_inc(self.s_norm.h, 1)))
                self.s_norm.inc()

            for h in range(NH):
                for qb in range(NQB):
                    K = 4 * qb + 4
                    i_hqb = self.ghqb
                    self.ghqb += 1
                    exp_done = {}
                    msk_done = {}
                    for pl0 in range(0, K // 2, 2):
                        group = []
                        for pl in (pl0, pl0 + 1):
                            j0, j1 = 2 * pl, 2 * pl + 1
                            gp = self.gpair
                            self.gpair += 1
                            sl2 = gp % 3
                            w_free = self.pss_free[sl2]
                            qcol = (2 * (h // 2)) * S
                            kcol = (2 * (h // 2) + 1) * S
                            hb = 64 * (h % 2)
                            w_head = (self.qkT_base + NH * NQB
                                      if (h == 0 and qb == 0 and pl == 0)
                                      else 0)

                            def _smm(e, qb=qb, j0=j0, sl2=sl2, wf=w_free,
                                     qcol=qcol, kcol=kcol, hb=hb, wh=w_head):
                                if wh:
                                    e.wait_ge(self.s_qkT.h, wh)
                                if wf:
                                    e.wait_ge(self.s_exp.h, wf)
                                e.matmul(pss[sl2][:, 0:512],
                                         self.ident[:, 0:128],
                                         self.ident[:, 128:640],
                                         start=True, stop=True,
                                         skip_group_check=True)
                                for jj in (j0, j0 + 1):
                                    e.matmul(pss[sl2][:, (jj % 2) * 512:
                                                      (jj % 2) * 512 + 512],
                                             qkT[hb:hb + 64, kcol + jj * 128:
                                                 kcol + (jj + 1) * 128],
                                             qkT[hb:hb + 64, qcol + qb * 512:
                                                 qcol + qb * 512 + 512],
                                             start=True, stop=True) \
                                     .then_inc(self.s_smm.h, 1)
                            self.add("tensor", _smm)
                            self.s_smm.inc(2)
                            group.append((pl, j0, j1, sl2, self.s_smm.n))

                        if pending is not None:
                            for (ph, pqb, pi, pK, pexp, pmsk, pjs) in pending:
                                plan_pv(ph, pqb, pi, pK, pexp, pmsk, pjs)
                                if pjs[-1] == pK - 1:
                                    plan_tail(ph, pqb, pi)

                        newpend = []
                        for (pl, j0, j1, sl2, w_smm) in group:
                            slot0 = j0 % 8
                            w_pt = max(self.pt_free[slot0],
                                       self.pt_free[slot0 + 1])

                            def _exp(e, sl2=sl2, slot0=slot0, w=w_smm, wp=w_pt):
                                e.wait_ge(self.s_smm.h, w)
                                if wp:
                                    e.wait_ge(self.s_pvmm.h, wp)
                                e.activation(PT[:, slot0 * 512:
                                                slot0 * 512 + 1024],
                                             pss[sl2][:, :], AF.Exp,
                                             bias=self.ecol[:, 1:2],
                                             scale=0.125) \
                                 .then_inc(self.s_exp.h, 1)
                            self.add("scalar", _exp)
                            v_exp = self.s_exp.inc()
                            self.pss_free[sl2] = v_exp
                            exp_done[pl] = v_exp
                            for j in (j0, j1):
                                if j >= 4 * qb:
                                    sigma = (j - 4 * qb) * 128
                                    col = (j % 8) * 512 + sigma

                                    def _msk(e, col=col, w=v_exp):
                                        e.wait_ge(self.s_exp.h, w)
                                        e.affine_select(
                                            PT[:, col:col + 128],
                                            PT[:, col:col + 128],
                                            pattern=[[1, 128]],
                                            compare_op=ALU.is_ge, fill=0.0,
                                            base=0, channel_multiplier=-1) \
                                         .then_inc(self.s_msk.h, 1)
                                    self.add("gpsimd", _msk)
                                    msk_done[j] = self.s_msk.inc()
                            newpend.append((h, qb, i_hqb, K, exp_done,
                                            msk_done, [j0, j1]))
                        pending = newpend
            if pending is not None:
                for (ph, pqb, pi, pK, pexp, pmsk, pjs) in pending:
                    plan_pv(ph, pqb, pi, pK, pexp, pmsk, pjs)
                    if pjs[-1] == pK - 1:
                        plan_tail(ph, pqb, pi)
            self.attnT_done = self.s_norm.n

    def p2b(self, s, sc2, attnT, x1):
        S, NT = self.S, self.NT
        with ExitStack() as sc:
            wob = self.sb(sc, f"wob{s}", [128, DC * D], BF16)
            xr = [self.sb(sc, f"xr{s}_{i}", [128, D], F32) for i in range(2)]
            psw = [self.ps(sc, f"psw{s}_{i}", [128, 1024], F32) for i in range(2)]
            pv_all = self.s_pvmm.n
            for c in range(DC):
                self.add("sync", lambda e, c=c, wp=pv_all if c == 0 else 0: (
                    e.wait_ge(self.s_pvmm.h, wp) if wp else None,
                    e.dma_start(out=wob[:, c * D:(c + 1) * D], in_=self.wo_b[c])
                    .then_inc(self.s_ldw2.h, 16)))
                self.s_ldw2.inc(16)
            v_wo = self.s_ldw2.n
            b_x1 = self.s_x1.n
            for t in range(NT):
                sl = t % 2
                lsem = self.s_ldx if sl == 0 else self.s_ldxb
                w_x1 = b_x1 + t - 1 if t >= 2 else 0
                w_self = lsem.n if t < 2 else 0
                v_ldx = lsem.inc(16)
                self.add("sync", lambda e, s_=s, t=t, sl=sl, ls=lsem,
                         w=w_x1, wsf=w_self: (
                    e.wait_ge(self.s_x1.h, w) if w else None,
                    e.wait_ge(ls.h, wsf) if wsf else None,
                    e.dma_start(out=xr[sl][:, :],
                                in_=self.x_hbm[s_, t * 128:(t + 1) * 128, :])
                     .then_inc(ls.h, 16)))

                def _womm(e, t=t, sl=sl, w=v_wo, w2=w_x1,
                          wa=self.attnT_done, first=(t == 0)):
                    if first:
                        e.wait_ge(self.s_ldw2.h, w)
                        e.wait_ge(self.s_norm.h, wa)
                    if w2:
                        e.wait_ge(self.s_x1.h, w2)
                    for c in range(DC):
                        for nb in range(2):
                            ins = e.matmul(
                                psw[sl][:, nb * 512:(nb + 1) * 512],
                                attnT[:, c * S + t * 128: c * S + (t + 1) * 128],
                                wob[:, c * D + nb * 512: c * D + (nb + 1) * 512],
                                start=(c == 0), stop=(c == DC - 1),
                                skip_group_check=True)
                    ins.then_inc(self.s_womm.h, 1)
                self.add("tensor", _womm)
                v_w = self.s_womm.inc()
                self.add("vector", lambda e, t=t, sl=sl, ls=lsem, w=v_w,
                         wl=v_ldx: (
                    e.wait_ge(self.s_womm.h, w),
                    e.wait_ge(ls.h, wl),
                    e.tensor_add(x1[:, t * D:(t + 1) * D], psw[sl][:, :],
                                 xr[sl][:, :]).then_inc(self.s_x1.h, 1)))
                self.s_x1.inc()

    def p3(self, s, sc2, x1):
        S, NT = self.S, self.NT
        with ExitStack() as sc:
            zT = self.sb(sc, f"zT{s}", [128, DC * S], BF16)
            wdbuf = [self.sb(sc, f"wdb{s}_{i}", [128, DC * HE], BF16)
                     for i in range(2)]
            wupbuf = [self.sb(sc, f"wub{s}_{i}", [HE, D], BF16)
                      for i in range(2)]
            wdsel = self.sb(sc, f"wds{s}", [128, DC * HE], BF16)
            wupsel = self.sb(sc, f"wus{s}", [HE, D], BF16)
            bdsel = self.sb(sc, f"bds{s}", [HE, 1], F32)
            selbc = self.sb(sc, f"selbc{s}", [128, 16], F32)
            bg = self.sb(sc, f"bg{s}", [128, D], F32)
            hT = self.sb(sc, f"hT{s}", [64, S], BF16)
            u2b = [self.sb(sc, f"u2{s}_{i}", [128, D], F32) for i in range(2)]
            t3b = [self.sb(sc, f"t3{s}_{i}", [128, D], F32) for i in range(2)]
            zb = [self.sb(sc, f"zb{s}_{i}", [128, D], BF16) for i in range(2)]
            gz = self.sb(sc, f"gz{s}", [128, D], BF16)
            st6 = [self.sb(sc, f"st6b{s}_{i}", [128, 12], F32)
                   for i in range(2)]
            mvA = [self.sb(sc, f"mvA{s}_{i}", [128, 2], F32) for i in range(2)]
            mvB = [self.sb(sc, f"mvB{s}_{i}", [128, 2], F32) for i in range(2)]
            rsA = [self.sb(sc, f"rsA{s}_{i}", [128, 1], F32) for i in range(2)]
            rsB = [self.sb(sc, f"rsB{s}_{i}", [128, 1], F32) for i in range(2)]
            rows = self.sb(sc, f"rows{s}", [1, D + 64], F32)
            crow4 = self.sb(sc, f"crow4_{s}", [1, 4 * D], F32)
            x1_all = self.s_x1.n
            self.add("sync", lambda e, wp=x1_all: (
                e.wait_ge(self.s_x1.h, wp),
                e.dma_start(out=crow4[:, :], in_=self.din["crow"][:, 0:4 * D])
                .then_inc(self.s_ldw2.h, 16)))
            v_crow = self.s_ldw2.inc(16)
            ohgcol = self.sb(sc, f"ohg{s}", [NE, 1], F32)
            zbnT_t = self.sb(sc, f"zbnT{s}", [128, NE], F32)
            t2buf = [self.sb(sc, f"t2o{s}_{i}", [128, D], F32) for i in range(2)]
            x2buf = [self.sb(sc, f"x2o{s}_{i}", [128, D], F32) for i in range(2)]

            R_zs = rows[:, 0:D]
            R_zbn = R_zs
            R_bup = R_zs
            o3 = D
            R_lg = rows[:, o3:o3 + NE]
            R_prob = rows[:, o3 + NE:o3 + 2 * NE]
            R_oh = rows[:, o3 + 2 * NE:o3 + 3 * NE]
            R_ohg = rows[:, o3 + 3 * NE:o3 + 4 * NE]
            nmx = rows[:, o3 + 4 * NE:o3 + 4 * NE + 1]
            sume = rows[:, o3 + 4 * NE + 1:o3 + 4 * NE + 2]
            rsum = rows[:, o3 + 4 * NE + 2:o3 + 4 * NE + 3]
            gate = rows[:, o3 + 4 * NE + 3:o3 + 4 * NE + 4]


            with ExitStack() as sa:
                psz = [self.ps(sa, f"psz{s}_{i}", [1, 512], F32) for i in range(2)]
                psztr = [self.ps(sa, f"pzt{s}_{i}", [128, 1024], BF16)
                         for i in range(2)]
                b_zmm = self.s_zmm.n
                b_z = self.s_z.n
                b_ztr = self.s_ztr.n
                b_zT = self.s_zT.n
                b_st3 = self.s_stat.n
                b_rs3 = self.s_rstd.n
                b_t3 = self.s_t3.n

                V1, V2, VR1, VR2 = {}, {}, {}, {}

                def q_s1(t):
                    sl = t % 2
                    c0 = self.s_ch.inc()
                    wp = V2.get(t - 2, 0)
                    V1[t] = self.s_stat.inc()
                    self.add("vector", lambda e, t=t, sl=sl, c0=c0, wp=wp: (
                        e.wait_ge(self.s_stat.h, wp) if wp else None,
                        e.bn_stats(st6[sl][:, 0:6], x1[:, t * D:t * D + 512]),
                        e.bn_stats(st6[sl][:, 6:12], x1[:, t * D + 512:(t + 1) * D])
                         .then_inc(self.s_ch.h, 1),
                        e.wait_ge(self.s_ch.h, c0),
                        e.bn_aggr(mvA[sl][:, :], st6[sl][:, :])
                         .then_inc(self.s_stat.h, 1)))

                def q_r1(t):
                    sl = t % 2
                    c1 = self.s_cha.inc()
                    VR1[t] = self.s_rstd.inc()
                    self.add("scalar", lambda e, sl=sl, w=V1[t],
                             c1=c1: (
                        e.wait_ge(self.s_stat.h, w),
                        e.activation(rsA[sl][:, :], mvA[sl][:, 1:2], AF.Ln,
                                     bias=self.ecol[:, 0:1])
                         .then_inc(self.s_cha.h, 1),
                        e.wait_ge(self.s_cha.h, c1),
                        e.activation(rsA[sl][:, :], rsA[sl][:, :], AF.Exp,
                                     scale=-0.5).then_inc(self.s_rstd.h, 1)))

                def q_s2(t):
                    sl = t % 2
                    c1 = self.s_ch.inc()
                    c2 = self.s_ch.inc()
                    V2[t] = self.s_stat.inc()
                    self.add("vector", lambda e, t=t, sl=sl, w=VR1[t],
                             c1=c1, c2=c2: (
                        e.wait_ge(self.s_rstd.h, w),
                        e.tensor_scalar(u2b[sl][:, :], x1[:, t * D:(t + 1) * D],
                                        mvA[sl][:, 0:1], rsA[sl][:, 0:1],
                                        ALU.subtract, ALU.mult)
                         .then_inc(self.s_ch.h, 1),
                        e.wait_ge(self.s_ch.h, c1),
                        e.bn_stats(st6[sl][:, 0:6], u2b[sl][:, 0:512]),
                        e.bn_stats(st6[sl][:, 6:12], u2b[sl][:, 512:1024])
                         .then_inc(self.s_ch.h, 1),
                        e.wait_ge(self.s_ch.h, c2),
                        e.bn_aggr(mvB[sl][:, :], st6[sl][:, :])
                         .then_inc(self.s_stat.h, 1)))

                def q_r2(t):
                    sl = t % 2
                    c1 = self.s_cha.inc()
                    VR2[t] = self.s_rstd.inc()
                    self.add("scalar", lambda e, sl=sl, w=V2[t],
                             c1=c1: (
                        e.wait_ge(self.s_stat.h, w),
                        e.activation(rsB[sl][:, :], mvB[sl][:, 1:2], AF.Ln,
                                     bias=self.ecol[:, 0:1])
                         .then_inc(self.s_cha.h, 1),
                        e.wait_ge(self.s_cha.h, c1),
                        e.activation(rsB[sl][:, :], rsB[sl][:, :], AF.Exp,
                                     scale=-0.5).then_inc(self.s_rstd.h, 1)))

                def q_t3(t):
                    sl = t % 2
                    w_zmm = b_zmm + t - 1 if t >= 2 else 0
                    w_z = b_z + t - 1 if t >= 2 else 0
                    self.s_t3.inc()
                    self.add("vector", lambda e, sl=sl, w=VR2[t],
                             w2=w_zmm, w3=w_z: (
                        e.wait_ge(self.s_rstd.h, w),
                        e.wait_ge(self.s_zmm.h, w2) if w2 else None,
                        e.wait_ge(self.s_z.h, w3) if w3 else None,
                        e.tensor_scalar(t3b[sl][:, :], u2b[sl][:, :],
                                        mvB[sl][:, 0:1], rsB[sl][:, 0:1],
                                        ALU.subtract, ALU.mult)
                         .then_inc(self.s_t3.h, 1)))

                def q_zmm(t):
                    sl = t % 2

                    def _zmm(e, t=t, sl=sl, w=b_t3 + t + 1):
                        e.wait_ge(self.s_t3.h, w)
                        for nb in range(2):
                            ins = e.matmul(psz[nb][:, :], self.ones[:, 0:1],
                                           t3b[sl][:, nb * 512:(nb + 1) * 512],
                                           start=(t == 0), stop=(t == NT - 1),
                                           skip_group_check=True)
                        ins.then_inc(self.s_zmm.h, 1)
                    self.add("tensor", _zmm)
                    self.s_zmm.inc()

                def q_z(t):
                    sl = t % 2
                    w_ztr = b_ztr + t - 1 if t >= 2 else 0
                    c1 = self.s_chg.inc()
                    w_own = b_z + t if t >= 1 else 0
                    self.add("gpsimd", lambda e, sl=sl, w=b_t3 + t + 1,
                             w2=w_ztr, c1=c1, wo=w_own: (
                        e.wait_ge(self.s_t3.h, w),
                        e.wait_ge(self.s_ztr.h, w2) if w2 else None,
                        e.wait_ge(self.s_z.h, wo) if wo else None,
                        e.tensor_mul(gz[:, :], t3b[sl][:, :], self.gtile[:, :])
                         .then_inc(self.s_chg.h, 1),
                        e.wait_ge(self.s_chg.h, c1),
                        e.tensor_add(zb[sl][:, :], gz[:, :], self.btile[:, :])
                         .then_inc(self.s_z.h, 1)))
                    self.s_z.inc()

                def q_ztr(t):
                    sl = t % 2
                    w_zT = b_zT + t - 1 if t >= 2 else 0

                    def _ztr(e, sl=sl, w=b_z + t + 1, w2=w_zT):
                        e.wait_ge(self.s_z.h, w)
                        if w2:
                            e.wait_ge(self.s_zT.h, w2)
                        for c in range(DC):
                            ins = e.transpose(psztr[sl][:, c * 128:(c + 1) * 128],
                                              zb[sl][:, c * 128:(c + 1) * 128],
                                              self.ident[:, 0:128])
                        ins.then_inc(self.s_ztr.h, 1)
                    self.add("tensor", _ztr)
                    self.s_ztr.inc()
                    dstap = (zT[:, :].rearrange("p (c q) -> p c q", q=S)
                             [:, :, t * 128:(t + 1) * 128])
                    srcap = psztr[sl][:, :].rearrange("p (c f) -> p c f", f=128)
                    self.add("vector", lambda e, d=dstap, sr=srcap,
                             w=b_ztr + t + 1: (
                        e.wait_ge(self.s_ztr.h, w),
                        e.tensor_copy(d, sr).then_inc(self.s_zT.h, 1)))
                    self.s_zT.inc()

                for t in range(NT):
                    q_s1(t)
                    q_r1(t)
                    if t >= 1:
                        q_s2(t - 1)
                        q_r2(t - 1)
                    if t >= 2:
                        q_t3(t - 2)
                        q_zmm(t - 2)
                        q_z(t - 2)
                        q_ztr(t - 2)
                q_s2(NT - 1)
                q_r2(NT - 1)
                for t in (NT - 2, NT - 1):
                    q_t3(t)
                    q_zmm(t)
                    q_z(t)
                    q_ztr(t)
                zT_done = self.s_zT.n
                zmm_done = self.s_zmm.n
                self.add("scalar", lambda e, w=zmm_done: (
                    e.wait_ge(self.s_zmm.h, w),
                    e.copy(R_zs[:, 0:512], psz[0][:, :]),
                    e.copy(R_zs[:, 512:1024], psz[1][:, :])
                     .then_inc(self.s_gat.h, 1)))
                v_g = self.s_gat.inc()

            with ExitStack() as sg:
                pzb = self.ps(sg, f"pzb{s}", [128, NE], F32)
                plg = self.ps(sg, f"plg{s}", [1, NE], F32)
                pohbc = self.ps(sg, f"poh{s}", [128, 16], F32)
                pbup = [self.ps(sg, f"pbu{s}_{i}", [1, 512], F32) for i in range(2)]
                pbg = self.ps(sg, f"pbg{s}", [128, 1024], F32)
                pohg = self.ps(sg, f"pog{s}", [NE, 1], F32)

                cb = self.s_ch.n
                self.s_ch.inc(5)

                def _zbn(e, w=v_g, cb=cb, wc=v_crow):
                    e.wait_ge(self.s_gat.h, w)
                    e.wait_ge(self.s_ldw2.h, wc)
                    e.tensor_scalar_mul(R_zs, R_zs, 1.0 / S).then_inc(self.s_ch.h, 1)
                    e.wait_ge(self.s_ch.h, cb + 1)
                    e.tensor_mul(R_zs, R_zs, crow4[0:1, 2 * D:3 * D]) \
                     .then_inc(self.s_ch.h, 1)
                    e.wait_ge(self.s_ch.h, cb + 2)
                    e.tensor_add(R_zs, R_zs, crow4[0:1, 3 * D:4 * D]) \
                     .then_inc(self.s_ch.h, 1)
                    e.wait_ge(self.s_ch.h, cb + 3)
                    e.tensor_sub(R_zbn, R_zs, crow4[0:1, 0:D]) \
                     .then_inc(self.s_ch.h, 1)
                    e.wait_ge(self.s_ch.h, cb + 4)
                    e.tensor_mul(R_zbn, R_zbn, self.alpha[0:1, :]) \
                     .then_inc(self.s_ch.h, 1)
                    e.wait_ge(self.s_ch.h, cb + 5)
                    e.tensor_add(R_zbn, R_zbn, crow4[0:1, D:2 * D]) \
                     .then_inc(self.s_gat.h, 1)
                self.add("vector", _zbn)
                v_g = self.s_gat.inc()

                def _zbnT(e, w=v_g):
                    e.wait_ge(self.s_gat.h, w)
                    for c in range(DC):
                        ins = e.transpose(pzb[:, c:c + 1],
                                          R_zbn[:, c * 128:(c + 1) * 128],
                                          self.ones[0:1, 0:1])
                    ins.then_inc(self.s_gat.h, 1)
                self.add("tensor", _zbnT)
                v_g = self.s_gat.inc()
                self.add("scalar", lambda e, w=v_g: (
                    e.wait_ge(self.s_gat.h, w),
                    e.copy(zbnT_t[:, :], pzb[:, :]).then_inc(self.s_gat.h, 1)))
                v_g = self.s_gat.inc()

                def _lg(e, w=v_g):
                    e.wait_ge(self.s_gat.h, w)
                    for c in range(DC):
                        ins = e.matmul(plg[:, :], zbnT_t[:, c:c + 1],
                                       self.wrsb[:, c * NE:(c + 1) * NE],
                                       start=(c == 0), stop=(c == DC - 1))
                    ins.then_inc(self.s_gat.h, 1)
                self.add("tensor", _lg)
                v_g = self.s_gat.inc()
                self.add("scalar", lambda e, w=v_g: (
                    e.wait_ge(self.s_gat.h, w),
                    e.copy(R_lg, plg[:, :]).then_inc(self.s_gat.h, 1)))
                v_g = self.s_gat.inc()

                def _sm1(e, w=v_g, c1=self.s_ch.inc()):
                    e.wait_ge(self.s_gat.h, w)
                    e.tensor_add(R_lg, R_lg, self.brsb[:, :]) \
                     .then_inc(self.s_ch.h, 1)
                    e.wait_ge(self.s_ch.h, c1)
                    e.reduce_max(nmx, R_lg, axis=AX.X, negate=True) \
                     .then_inc(self.s_gat.h, 1)
                self.add("vector", _sm1)
                v_g = self.s_gat.inc()
                self.add("scalar", lambda e, w=v_g: (
                    e.wait_ge(self.s_gat.h, w),
                    e.activation(R_prob, R_lg, AF.Exp, bias=nmx,
                                 accum_out=sume).then_inc(self.s_gat.h, 1)))
                v_g = self.s_gat.inc()

                def _sm2(e, w=v_g):
                    e.wait_ge(self.s_gat.h, w)
                    e.reciprocal(rsum, sume).then_inc(self.s_gat.h, 1)
                self.add("vector", _sm2)
                v_g = self.s_gat.inc()

                def _sm3(e, w=v_g):
                    e.wait_ge(self.s_gat.h, w)
                    e.tensor_single_scalar(R_prob, R_prob, rsum, ALU.mult) \
                     .then_inc(self.s_gat.h, 1)
                self.add("vector", _sm3)
                v_g = self.s_gat.inc()

                def _sm4(e, w=v_g):
                    e.wait_ge(self.s_gat.h, w)
                    e.reduce_max(gate, R_prob, axis=AX.X).then_inc(self.s_gat.h, 1)
                self.add("vector", _sm4)
                v_g = self.s_gat.inc()

                def _sm5(e, w=v_g):
                    e.wait_ge(self.s_gat.h, w)
                    e.tensor_single_scalar(R_oh, R_prob, gate, ALU.is_equal) \
                     .then_inc(self.s_gat.h, 1)
                self.add("vector", _sm5)
                v_g = self.s_gat.inc()

                def _sm6(e, w=v_g):
                    e.wait_ge(self.s_gat.h, w)
                    e.tensor_single_scalar(R_ohg, R_oh, gate, ALU.mult) \
                     .then_inc(self.s_gat.h, 1)
                self.add("vector", _sm6)
                v_g = self.s_gat.inc()

                def _bc(e, w=v_g):
                    e.wait_ge(self.s_gat.h, w)
                    e.matmul(pohbc[:, 0:NE], self.ones[0:1, :], R_oh,
                             start=True, stop=True)
                    e.matmul(pohbc[:, NE:2 * NE], self.ones[0:1, :], R_ohg,
                             start=True, stop=True, skip_group_check=True)
                    e.transpose(pohg[:, :], R_ohg, self.ones[0:1, 0:1]) \
                     .then_inc(self.s_gat.h, 1)
                self.add("tensor", _bc)
                v_g = self.s_gat.inc()
                self.add("scalar", lambda e, w=v_g: (
                    e.wait_ge(self.s_gat.h, w),
                    e.copy(selbc[:, :], pohbc[:, :]),
                    e.copy(ohgcol[:, :], pohg[:, :]).then_inc(self.s_gat.h, 1)))
                v_g = self.s_gat.inc()

                def _bup(e, w=v_g):
                    e.wait_ge(self.s_gat.h, w)
                    e.matmul(pbup[0][:, :], ohgcol[:, :],
                             self.bupall[:, 0:512], start=True, stop=True)
                    e.matmul(pbup[1][:, :], ohgcol[:, :],
                             self.bupall[:, 512:1024], start=True,
                             stop=True, skip_group_check=True) \
                     .then_inc(self.s_gat.h, 1)
                self.add("tensor", _bup)
                v_g = self.s_gat.inc()
                self.add("scalar", lambda e, w=v_g: (
                    e.wait_ge(self.s_gat.h, w),
                    e.copy(R_bup[:, 0:512], pbup[0][:, :]),
                    e.copy(R_bup[:, 512:1024], pbup[1][:, :])
                     .then_inc(self.s_gat.h, 1)))
                v_g = self.s_gat.inc()

                def _bg(e, w=v_g):
                    e.wait_ge(self.s_gat.h, w)
                    for nb in range(2):
                        ins = e.matmul(pbg[:, nb * 512:(nb + 1) * 512],
                                       self.ones[0:1, :],
                                       R_bup[:, nb * 512:(nb + 1) * 512],
                                       start=True, stop=True,
                                       skip_group_check=True)
                    ins.then_inc(self.s_gat.h, 1)
                self.add("tensor", _bg)
                v_g = self.s_gat.inc()
                self.add("vector", lambda e, w=v_g: (
                    e.wait_ge(self.s_gat.h, w),
                    e.tensor_copy(bg[:, :], pbg[:, :]).then_inc(self.s_gat.h, 1)))
                v_bg = self.s_gat.inc()

                b_sel = self.s_sel.n
                for ei in range(NE):
                    sl = ei % 2
                    esem = self.s_ldw3 if sl == 0 else self.s_ldw3b
                    w_gp = b_sel + ei - 1 if ei >= 2 else 0
                    w_self = esem.n if ei < 2 else 0
                    v_lw = esem.inc(32)
                    self.add("sync", lambda e, ei=ei, sl=sl, es=esem, w=w_gp,
                             wsf=w_self, wx=x1_all if ei < 2 else 0: (
                        e.wait_ge(self.s_sel.h, w) if w else None,
                        e.wait_ge(es.h, wsf) if wsf else None,
                        e.wait_ge(self.s_x1.h, wx) if wx else None,
                        e.dma_start(out=wdbuf[sl][:, :],
                                    in_=self.wd_b[:, ei * DC * HE:
                                                  (ei + 1) * DC * HE])
                         .then_inc(es.h, 16),
                        e.dma_start(out=wupbuf[sl][:, :],
                                    in_=self.wup_b[:, ei * D:(ei + 1) * D])
                         .then_inc(es.h, 16)))

                    def _sel(e, ei=ei, sl=sl, es=esem, w=v_lw, wg=v_bg,
                             cprev=self.s_sel.n):
                        if ei == 0:
                            e.wait_ge(self.s_gat.h, wg)
                        else:
                            e.wait_ge(self.s_sel.h, cprev)
                        e.wait_ge(es.h, w)
                        if ei == 0:
                            e.tensor_scalar(wdsel[:, :], wdbuf[sl][:, :],
                                            selbc[:, 0:1], None, ALU.mult)
                            e.tensor_scalar(bdsel[:, :], self.bdall[:, 0:1],
                                            selbc[0:HE, 0:1], None, ALU.mult)
                            ins = e.tensor_scalar(
                                wupsel[:, :], wupbuf[sl][:, :],
                                selbc[0:HE, NE:NE + 1], None, ALU.mult)
                        else:
                            e.scalar_tensor_tensor(
                                wdsel[:, :], wdbuf[sl][:, :],
                                selbc[:, ei:ei + 1], wdsel[:, :],
                                ALU.mult, ALU.add)
                            e.scalar_tensor_tensor(
                                bdsel[:, :], self.bdall[:, ei:ei + 1],
                                selbc[0:HE, ei:ei + 1], bdsel[:, :],
                                ALU.mult, ALU.add)
                            ins = e.scalar_tensor_tensor(
                                wupsel[:, :], wupbuf[sl][:, :],
                                selbc[0:HE, NE + ei:NE + ei + 1],
                                wupsel[:, :], ALU.mult, ALU.add)
                        ins.then_inc(self.s_sel.h, 1)
                    self.add("vector", _sel)
                    self.s_sel.inc()
                v_sel = self.s_sel.n

            with ExitStack() as sy:
                psh = [self.ps(sy, f"psh{s}_{i}", [HE, 512], F32)
                       for i in range(2)]
                psy = [self.ps(sy, f"psy{s}_{i}", [128, 1024], F32)
                       for i in range(2)]
                NB = S // 512
                b_hr = self.s_hrelu.n
                for tb in range(NB):
                    sl = tb % 2
                    w_hr = b_hr + tb - 1 if tb >= 2 else 0

                    def _hmm(e, tb=tb, sl=sl, w=v_sel, w2=w_hr, wz=zT_done,
                             first=(tb == 0)):
                        if first:
                            e.wait_ge(self.s_sel.h, w)
                            e.wait_ge(self.s_zT.h, wz)
                        if w2:
                            e.wait_ge(self.s_hrelu.h, w2)
                        for c in range(DC):
                            ins = e.matmul(
                                psh[sl][:, :],
                                wdsel[:, c * HE:(c + 1) * HE],
                                zT[:, c * S + tb * 512: c * S + tb * 512 + 512],
                                start=(c == 0), stop=(c == DC - 1))
                        ins.then_inc(self.s_ymm.h, 1)
                    self.add("tensor", _hmm)
                    v_h = self.s_ymm.inc()
                    self.add("scalar", lambda e, tb=tb, sl=sl, w=v_h, ws=v_sel: (
                        e.wait_ge(self.s_ymm.h, w),
                        e.wait_ge(self.s_sel.h, ws),
                        e.activation(hT[:, tb * 512:(tb + 1) * 512],
                                     psh[sl][:, :], AF.Relu,
                                     bias=bdsel[:, 0:1])
                         .then_inc(self.s_hrelu.h, 1)))
                    self.s_hrelu.inc()
                b_t2 = self.s_t2.n
                b_x2 = self.s_x2.n
                for t in range(NT):
                    sl = t % 2
                    w_hr = b_hr + t // 4 + 1
                    w_t2 = b_t2 + t - 1 if t >= 2 else 0

                    def _ymm(e, t=t, sl=sl, w=w_hr, w2=w_t2):
                        e.wait_ge(self.s_hrelu.h, w)
                        if w2:
                            e.wait_ge(self.s_t2.h, w2)
                        for nb in range(2):
                            ins = e.matmul(
                                psy[sl][:, nb * 512:(nb + 1) * 512],
                                hT[:, t * 128:(t + 1) * 128],
                                wupsel[:, nb * 512:(nb + 1) * 512],
                                start=True, stop=True, skip_group_check=True)
                        ins.then_inc(self.s_ymm.h, 1)
                    self.add("tensor", _ymm)
                    v_y = self.s_ymm.inc()
                    w_x2 = b_x2 + t - 1 if t >= 2 else 0
                    self.add("vector", lambda e, sl=sl, w=v_y, wb=v_bg, w2=w_x2: (
                        e.wait_ge(self.s_ymm.h, w),
                        e.wait_ge(self.s_gat.h, wb),
                        e.wait_ge(self.s_x2.h, w2) if w2 else None,
                        e.tensor_add(t2buf[sl][:, :], psy[sl][:, :], bg[:, :])
                         .then_inc(self.s_t2.h, 1)))
                    v_t2 = self.s_t2.inc()
                    ssem = self.s_st if t % 2 == 0 else self.s_st2
                    w_st = ssem.n if t >= 2 else 0
                    self.add("gpsimd", lambda e, t=t, sl=sl, ss=ssem, w=v_t2,
                             w2=w_st: (
                        e.wait_ge(self.s_t2.h, w),
                        e.wait_ge(ss.h, w2) if w2 else None,
                        e.tensor_add(x2buf[sl][:, :], t2buf[sl][:, :],
                                     x1[:, t * D:(t + 1) * D])
                         .then_inc(self.s_x2.h, 1)))
                    v_x2 = self.s_x2.inc()
                    self.add("sync", lambda e, s_=s, t=t, sl=sl, ss=ssem, w=v_x2: (
                        e.wait_ge(self.s_x2.h, w),
                        e.dma_start(out=self.out_hbm[s_, t * 128:(t + 1) * 128, :],
                                    in_=x2buf[sl][:, :])
                         .then_inc(ss.h, 16)))
                    ssem.inc(16)


def _host_inputs(xs, W_qkv, W_o, ln_g, ln_b, bn_g, bn_b, bn_mean, bn_var,
                 Wr, br, W_down, b_down, W_up, b_up, S):
    d = {}
    d["x"] = np.ascontiguousarray(xs, dtype=np.float32)

    wqk = np.empty((NH, DC, 128, 128), np.float32)
    for i in range(NH // 2):
        qrows = np.concatenate([W_qkv[(2 * i) * 64 + PERM64],
                                W_qkv[(2 * i + 1) * 64 + PERM64]], axis=0)
        krows = np.concatenate([W_qkv[D + (2 * i) * 64 + PERM64],
                                W_qkv[D + (2 * i + 1) * 64 + PERM64]], axis=0)
        for c in range(DC):
            wqk[2 * i, c] = qrows[:, c * 128:(c + 1) * 128].T
            wqk[2 * i + 1, c] = krows[:, c * 128:(c + 1) * 128].T
    d["wqk"] = wqk

    wv = np.empty((DC, 128, D), np.float32)
    wo = np.empty((DC, 128, D), np.float32)
    for c in range(DC):
        wv[c] = W_qkv[2 * D:3 * D, c * 128:(c + 1) * 128].T
        wo[c] = W_o[:, c * 128:(c + 1) * 128].T
    d["wv"] = wv
    d["wo"] = wo

    wd = np.transpose(W_down.reshape(NE, DC, 128, HE), (2, 0, 1, 3))
    d["wd"] = np.ascontiguousarray(wd.reshape(128, NE * DC * HE), np.float32)
    d["wup"] = np.ascontiguousarray(
        np.transpose(W_up, (1, 0, 2)).reshape(HE, NE * D), np.float32)

    pos = np.arange(S, dtype=np.float64)
    inv = THETA ** (-np.arange(0, DK, 2, dtype=np.float64) / DK)
    A64 = np.empty((64, S), np.float64)
    B64 = np.empty((64, S), np.float64)
    for blk in range(2):
        pair = np.arange(16 * blk, 16 * blk + 16)
        ang = pos[None, :] * inv[pair][:, None]
        A64[blk * 32:blk * 32 + 16] = np.cos(ang)
        A64[blk * 32 + 16:blk * 32 + 32] = np.cos(ang)
        B64[blk * 32:blk * 32 + 16] = -np.sin(ang)
        B64[blk * 32 + 16:blk * 32 + 32] = np.sin(ang)
    d["ropeA"] = np.tile(A64, (2, 1)).astype(np.float32)
    d["ropeB"] = np.tile(B64, (2, 1)).astype(np.float32)

    d["gtile"] = np.tile(ln_g[None, :], (128, 1)).astype(np.float32)
    d["btile"] = np.tile(ln_b[None, :], (128, 1)).astype(np.float32)
    import ml_dtypes
    d["ident"] = np.concatenate([np.eye(128), np.zeros((128, 512))],
                                axis=1).astype(ml_dtypes.bfloat16)
    d["ones"] = np.ones((128, 128), np.float32)
    d["crow"] = np.concatenate(
        [bn_mean, bn_b, ln_g, ln_b, bn_var, bn_g]).reshape(1, 6 * D).astype(np.float32)
    d["wr"] = np.ascontiguousarray(
        np.transpose(Wr.reshape(DC, 128, NE), (1, 0, 2)).reshape(128, DC * NE),
        np.float32)
    d["br"] = br.reshape(1, NE).astype(np.float32)
    d["bdall"] = np.ascontiguousarray(b_down.T, np.float32)
    d["bupall"] = np.ascontiguousarray(b_up, np.float32)
    d["ecol"] = np.tile(np.array([[EPS, -CSHIFT]], np.float32), (128, 1))
    return d


def kernel(L0_attn_x, L0_attn_W_qkv, L0_attn_W_o, L0_ffn_ln_g, L0_ffn_ln_b,
           L0_ffn_bn_g, L0_ffn_bn_b, L0_ffn_bn_mean, L0_ffn_bn_var,
           L0_ffn_Wr, L0_ffn_br, L0_ffn_W_down, L0_ffn_b_down, L0_ffn_W_up,
           L0_ffn_b_up):
    Bfull, S, _ = L0_attn_x.shape
    B_core = Bfull // NCORES
    b = Builder(B_core, S)
    nc = b.build()
    in_maps = []
    for core in range(NCORES):
        xs = L0_attn_x[core * B_core:(core + 1) * B_core]
        in_maps.append(_host_inputs(
            xs, L0_attn_W_qkv, L0_attn_W_o, L0_ffn_ln_g, L0_ffn_ln_b,
            L0_ffn_bn_g, L0_ffn_bn_b, L0_ffn_bn_mean, L0_ffn_bn_var,
            L0_ffn_Wr, L0_ffn_br, L0_ffn_W_down, L0_ffn_b_down, L0_ffn_W_up,
            L0_ffn_b_up, S))
    res = run_bass_kernel_spmd(nc, in_maps, core_ids=list(range(NCORES)))
    out = np.concatenate([r["out"] for r in res.results], axis=0)
    return out.astype(np.float32)


# version 13 (final)
# speedup vs baseline: 1.0344x; 1.0176x over previous
from collections import defaultdict
from contextlib import ExitStack

import numpy as np

import concourse.bass as bass
import concourse.mybir as mybir
from concourse.bass_utils import run_bass_kernel_spmd

F32 = mybir.dt.float32
BF16 = mybir.dt.bfloat16
AF = mybir.ActivationFunctionType
ALU = mybir.AluOpType
AX = mybir.AxisListType

D = 1024
NH = 16
DK = 64
NE = 8
HE = 64
THETA = 10000.0
EPS = 1e-5
CSHIFT = 24.0
NCORES = 8
DC = D // 128

PERM64 = []
for _blk in range(2):
    PERM64 += [2 * (16 * _blk + j) for j in range(16)]
    PERM64 += [2 * (16 * _blk + j) + 1 for j in range(16)]
PERM64 = np.array(PERM64)
SHUF16 = list(range(16, 32)) + list(range(16))


class Sem:
    def __init__(self, stack, nc, name):
        self.h = stack.enter_context(nc.semaphore(name))
        self.n = 0

    def inc(self, amt=1):
        self.n += amt
        return self.n


class Builder:
    ENGS = ("sync", "tensor", "scalar", "vector", "gpsimd")

    def __init__(self, B_core, S):
        assert S % 512 == 0
        self.BC, self.S = B_core, S
        self.NT = S // 128
        self.NQB = S // 512
        self.nc = bass.Bass("TRN2", target_bir_lowering=False)
        self.plan = defaultdict(list)
        self.st = ExitStack()
        self.gpair = 0
        self.ghqb = 0
        self.pt_free = defaultdict(int)
        self.pss_free = defaultdict(int)

    def add(self, eng, fn):
        self.plan[eng].append(fn)

    def flush(self):
        plan, self.plan = self.plan, defaultdict(list)
        if not any(plan.values()):
            return
        with self.nc.Block() as blk:
            for eng in self.ENGS:
                if plan[eng]:
                    getattr(blk, eng)(
                        lambda e, fns=tuple(plan[eng]): [f(e) for f in fns])

    def warm(self, psum_ap, n):
        def _w(e, n=n, p=psum_ap):
            for _ in range(n):
                e.matmul(p, self.ident[:, 0:128], self.ident[:, 128:640],
                         start=True, stop=True, skip_group_check=True)
        self.add("tensor", _w)

    def sb(self, stack, name, shape, dtype, side="left"):
        return stack.enter_context(
            self.nc.sbuf_tensor(name, shape, dtype, side=side))

    def ps(self, stack, name, shape, dtype):
        return stack.enter_context(self.nc.psum_tensor(name, shape, dtype))

    def build(self):
        nc, st = self.nc, self.st
        S, BC = self.S, self.BC

        self.x_hbm = nc.dram_tensor("x", [BC, S, D], F32, kind="ExternalInput").ap()
        self.out_hbm = nc.dram_tensor("out", [BC, S, D], F32,
                                      kind="ExternalOutput").ap()
        dram_in = [
            ("wqk", [NH, DC, 128, 128]), ("wv", [DC, 128, D]), ("wo", [DC, 128, D]),
            ("wd", [128, NE * DC * HE]), ("wup", [HE, NE * D]),
            ("ropeA", [128, S]), ("ropeB", [128, S]),
            ("gtile", [128, D]), ("btile", [128, D]), ("ones", [128, 128]),
            ("crow", [1, 6 * D]),
            ("wr", [128, DC * NE]), ("br", [1, NE]),
            ("bdall", [HE, NE]), ("bupall", [NE, D]), ("ecol", [128, 2]),
        ]
        self.din = {n: nc.dram_tensor(n, sh, F32, kind="ExternalInput").ap()
                    for n, sh in dram_in}
        self.din["ident"] = nc.dram_tensor("ident", [128, 640], BF16,
                                           kind="ExternalInput").ap()
        self.wqk_b = nc.dram_tensor("wqk_b", [NH, DC, 128, 128], BF16).ap()
        self.wv_b = nc.dram_tensor("wv_b", [DC, 128, D], BF16).ap()
        self.wo_b = nc.dram_tensor("wo_b", [DC, 128, D], BF16).ap()
        self.wd_b = nc.dram_tensor("wd_b", [128, NE * DC * HE], BF16).ap()
        self.wup_b = nc.dram_tensor("wup_b", [HE, NE * D], BF16).ap()

        with st:
            for name in ("c0 w0 ldx ldw ldw2 stat rstd u1 tr u1T qk rope qkT "
                         "vmm vev smm exp msk pvmm lrow bc norm womm x1 st ch w0b st2 "
                         "ldxb ldwb ldw3 ldw3b cha chg "
                         "t3 zmm z ztr zT gat sel hrelu ymm t2 x2 misc macc"
                         ).split():
                setattr(self, "s_" + name, Sem(st, nc, name))

            self.ropeA = self.sb(st, "ropeA_s", [128, S], F32)
            self.ropeB = self.sb(st, "ropeB_s", [128, S], F32)
            self.gtile = self.sb(st, "gtile_s", [128, D], F32)
            self.btile = self.sb(st, "btile_s", [128, D], F32)
            self.ident = self.sb(st, "ident_s", [128, 640], BF16)
            self.ones = self.sb(st, "ones_s", [128, 128], F32)
            self.wrsb = self.sb(st, "wr_s", [128, DC * NE], F32)
            self.brsb = self.sb(st, "br_s", [1, NE], F32)
            self.bdall = self.sb(st, "bd_s", [HE, NE], F32)
            self.bupall = self.sb(st, "bup_s", [NE, D], F32)
            self.alpha = self.sb(st, "alpha_s", [1, D], F32)
            self.ecol = self.sb(st, "ecol_s", [128, 2], F32)

            self.phase0()
            for s in range(BC):
                self.seq(s)

            self.add("sync", lambda e, w=self.s_st.n, w2=self.s_st2.n: (
                e.wait_ge(self.s_st.h, w), e.wait_ge(self.s_st2.h, w2)))
            self.flush()
        return nc

    def phase0(self):
        consts = [
            (self.ropeA, "ropeA"), (self.ropeB, "ropeB"),
            (self.gtile, "gtile"), (self.btile, "btile"),
            (self.ident, "ident"), (self.ones, "ones"),
            (self.wrsb, "wr"), (self.brsb, "br"),
            (self.bdall, "bdall"), (self.bupall, "bupall"),
            (self.ecol, "ecol"),
        ]
        for dst, name in consts:
            src = self.din[name]
            self.add("sync", lambda e, a=dst, b=src:
                     e.dma_start(out=a[:, :], in_=b[:, :]).then_inc(self.s_c0.h, 16))
            self.s_c0.inc(16)


        casts = []
        for h in range(NH):
            casts.append((self.din["wqk"][h].rearrange("c d e -> d c e"),
                          self.wqk_b[h].rearrange("c d e -> d c e"),
                          128, DC * 128))
        for c in range(DC):
            casts.append((self.din["wv"][c], self.wv_b[c], 128, D))
            casts.append((self.din["wo"][c], self.wo_b[c], 128, D))
        for q in range(4):
            n = NE * DC * HE // 4
            casts.append((self.din["wd"][:, q * n:(q + 1) * n],
                          self.wd_b[:, q * n:(q + 1) * n], 128, n))
        for q in range(4):
            n = NE * D // 4
            casts.append((self.din["wup"][:, q * n:(q + 1) * n],
                          self.wup_b[:, q * n:(q + 1) * n], HE, n))

        with ExitStack() as sc:
            cbuf_f = self.sb(sc, "cbuf_f", [128, 4096], F32)
            cbuf_b = self.sb(sc, "cbuf_b", [128, 4096], BF16)
            cvg = self.sb(sc, "cvg", [1, 2 * D], F32)
            self.add("sync", lambda e: e.dma_start(
                out=cvg[:, :], in_=self.din["crow"][:, 4 * D:6 * D])
                .then_inc(self.s_c0.h, 16))
            cw = self.s_c0.inc(16)

            c1 = self.s_cha.inc()

            def _alpha_act(e, w=cw, c1=c1):
                e.wait_ge(self.s_c0.h, w)
                e.activation(self.alpha[0:1, :], cvg[0:1, 0:D], AF.Ln,
                             bias=self.ecol[0:1, 0:1]).then_inc(self.s_cha.h, 1)
                e.wait_ge(self.s_cha.h, c1)
                e.activation(self.alpha[0:1, :], self.alpha[0:1, :], AF.Exp,
                             scale=-0.5).then_inc(self.s_misc.h, 1)
            self.add("scalar", _alpha_act)
            wm = self.s_misc.inc()

            def _alpha_dve(e, w=wm):
                e.wait_ge(self.s_misc.h, w)
                e.tensor_mul(self.alpha[0:1, :], self.alpha[0:1, :],
                             cvg[0:1, D:2 * D]).then_inc(self.s_misc.h, 1)
            self.add("vector", _alpha_dve)
            self.s_misc.inc()
            wsem = [self.s_w0, self.s_w0b]
            for i, (src, dst, npart, nf) in enumerate(casts):
                assert nf <= 2048
                off = (i % 2) * 2048
                bf = cbuf_f[0:npart, off:off + nf]
                bb = cbuf_b[0:npart, off:off + nf]
                if len(src.shape) == 3:
                    bf = bf.rearrange("p (c e) -> p c e", e=src.shape[2])
                    bb = bb.rearrange("p (c e) -> p c e", e=src.shape[2])
                ws = wsem[i % 2]
                w_st = ws.n if i >= 2 else 0
                w_ld = self.s_ldw.inc(16)
                self.add("sync", lambda e, s=src, bf=bf, ws=ws, w=w_st: (
                    e.wait_ge(ws.h, w) if w else None,
                    e.dma_start(out=bf, in_=s).then_inc(self.s_ldw.h, 16)))
                v_cast = self.s_macc.inc()
                self.add("vector", lambda e, bf=bf, bb=bb, ws=ws, w=w_ld, w2=w_st: (
                    e.wait_ge(self.s_ldw.h, w),
                    e.wait_ge(ws.h, w2) if w2 else None,
                    e.tensor_copy(bb, bf).then_inc(self.s_macc.h, 1)))
                self.add("sync", lambda e, d=dst, bb=bb, ws=ws, w=v_cast: (
                    e.wait_ge(self.s_macc.h, w),
                    e.dma_start(out=d, in_=bb).then_inc(ws.h, 16)))
                ws.inc(16)
            self.add("sync", lambda e, w=self.s_w0.n, w2=self.s_w0b.n: (
                e.wait_ge(self.s_w0.h, w), e.wait_ge(self.s_w0b.h, w2)))
            self.add("gpsimd", lambda e, w=self.s_c0.n: e.wait_ge(self.s_c0.h, w))
            self.add("vector", lambda e, w=self.s_misc.n: e.wait_ge(self.s_misc.h, w))
            self.flush()

    def seq(self, s):
        S, NT = self.S, self.NT
        scA = ExitStack()
        scQ = ExitStack()
        try:
            self.psq = [self.ps(scQ, f"psq{s}_{i}", [128, 512], F32)
                        for i in range(2)]
            with ExitStack() as sc1:
                qkT = self.sb(sc1, f"qkT{s}", [128, NH * S], BF16)
                vsb = self.sb(sc1, f"v{s}", [128, NT * NH * 66], BF16)
                self.p1(s, sc1, qkT, vsb)
                scQ.close()
                attnT = self.sb(scA, f"attnT{s}", [128, DC * S], BF16,
                                side="right")
                self.p2a(s, sc1, qkT, vsb, attnT)
            with ExitStack() as sc2:
                x1 = self.sb(sc2, f"x1_{s}", [128, NT * D], F32)
                self.p2b(s, sc2, attnT, x1)
                scA.close()
                self.p3(s, sc2, x1)
            self.flush()
        finally:
            scA.close()
            scQ.close()

    def p1(self, s, sc1, qkT, vsb):
        S, NT, NQB = self.S, self.NT, self.NQB
        with ExitStack() as sc:
            u1T = self.sb(sc, f"u1T_{s}", [128, DC * S], BF16)
            wvb = self.sb(sc, f"wvb{s}", [128, DC * D], BF16)

            v_vset = self.s_misc.inc()
            self.add("gpsimd", lambda e: e.memset(vsb[:, :], 1.0)
                     .then_inc(self.s_misc.h, 1))
            for c in range(DC):
                self.add("sync", lambda e, c=c: e.dma_start(
                    out=wvb[:, c * D:(c + 1) * D], in_=self.wv_b[c])
                    .then_inc(self.s_ldw2.h, 16))
                self.s_ldw2.inc(16)
            v_wv = self.s_ldw2.n

            with ExitStack() as sca:
                xbuf = [self.sb(sca, f"xb{s}_{i}", [128, D], F32) for i in range(2)]
                u1buf = [self.sb(sca, f"u1b{s}_{i}", [128, D], BF16) for i in range(2)]
                stat6 = [self.sb(sca, f"st6_{s}_{i}", [128, 12], F32)
                         for i in range(2)]
                mv = [self.sb(sca, f"mv_{s}_{i}", [128, 2], F32) for i in range(2)]
                rstd = [self.sb(sca, f"rs_{s}_{i}", [128, 1], F32)
                        for i in range(2)]
                ptr = [self.ps(sca, f"ptr{s}_0", [128, 1024], BF16)] * 2
                psv = [self.ps(sca, f"psv{s}_0", [128, 1024], F32)] * 2
                b_u1 = self.s_u1.n
                b_tr = self.s_tr.n
                b_ev = self.s_u1T.n
                b_stat = self.s_stat.n
                b_rstd = self.s_rstd.n
                b_vev = self.s_vev.n

                def st_load(t):
                    sl = t % 2
                    lsem = self.s_ldx if sl == 0 else self.s_ldxb
                    w_u1 = b_u1 + t - 1 if t >= 2 else 0
                    w_self = lsem.n if t < 2 else 0
                    wst = (self.s_st.n, self.s_st2.n) if t < 2 else (0, 0)
                    lsem.inc(16)
                    self.add("sync", lambda e, s_=s, t=t, sl=sl, ls=lsem,
                             w=w_u1, wsf=w_self, wst=wst: (
                        e.wait_ge(self.s_u1.h, w) if w else None,
                        e.wait_ge(ls.h, wsf) if wsf else None,
                        e.wait_ge(self.s_st.h, wst[0]) if wst[0] else None,
                        e.wait_ge(self.s_st2.h, wst[1]) if wst[1] else None,
                        e.dma_start(out=xbuf[sl][:, :],
                                    in_=self.x_hbm[s_, t * 128:(t + 1) * 128, :])
                         .then_inc(ls.h, 16)))

                def st_stats(t):
                    sl = t % 2
                    lsem = self.s_ldx if sl == 0 else self.s_ldxb
                    w_ld = lsem.n
                    c1 = self.s_ch.inc()
                    self.s_stat.inc()
                    self.add("vector", lambda e, sl=sl, ls=lsem, w=w_ld, c1=c1: (
                        e.wait_ge(ls.h, w),
                        e.bn_stats(stat6[sl][:, 0:6], xbuf[sl][:, 0:512]),
                        e.bn_stats(stat6[sl][:, 6:12], xbuf[sl][:, 512:1024])
                         .then_inc(self.s_ch.h, 1),
                        e.wait_ge(self.s_ch.h, c1),
                        e.bn_aggr(mv[sl][:, :], stat6[sl][:, :])
                         .then_inc(self.s_stat.h, 1)))

                def st_rstd(t):
                    sl = t % 2
                    c1 = self.s_cha.inc()
                    self.s_rstd.inc()
                    self.add("scalar", lambda e, sl=sl, w=b_stat + t + 1, c1=c1: (
                        e.wait_ge(self.s_stat.h, w),
                        e.activation(rstd[sl][:, :], mv[sl][:, 1:2], AF.Ln,
                                     bias=self.ecol[:, 0:1])
                         .then_inc(self.s_cha.h, 1),
                        e.wait_ge(self.s_cha.h, c1),
                        e.activation(rstd[sl][:, :], rstd[sl][:, :], AF.Exp,
                                     scale=-0.5).then_inc(self.s_rstd.h, 1)))

                def st_apply(t):
                    sl = t % 2
                    w_tr = b_tr + t - 1 if t >= 2 else 0
                    self.s_u1.inc()
                    self.add("vector", lambda e, sl=sl, w=b_rstd + t + 1,
                             w2=w_tr: (
                        e.wait_ge(self.s_rstd.h, w),
                        e.wait_ge(self.s_tr.h, w2) if w2 else None,
                        e.tensor_scalar(u1buf[sl][:, :], xbuf[sl][:, :],
                                        mv[sl][:, 0:1], rstd[sl][:, 0:1],
                                        ALU.subtract, ALU.mult)
                         .then_inc(self.s_u1.h, 1)))

                def st_tr(t):
                    sl = t % 2
                    w_ev = b_ev + t if t >= 1 else 0
                    self.s_tr.inc()

                    def _tr(e, sl=sl, w=b_u1 + t + 1, w2=w_ev):
                        e.wait_ge(self.s_u1.h, w)
                        if w2:
                            e.wait_ge(self.s_u1T.h, w2)
                        for c in range(DC):
                            ins = e.transpose(ptr[sl][:, c * 128:(c + 1) * 128],
                                              u1buf[sl][:, c * 128:(c + 1) * 128],
                                              self.ident[:, 0:128])
                        ins.then_inc(self.s_tr.h, 1)
                    self.add("tensor", _tr)

                def st_evac(t):
                    sl = t % 2
                    self.s_u1T.inc()
                    dstap = (u1T[:, :].rearrange("p (c q) -> p c q", q=S)
                             [:, :, t * 128:(t + 1) * 128])
                    srcap = ptr[sl][:, :].rearrange("p (c f) -> p c f", f=128)
                    self.add("vector", lambda e, d=dstap, sr=srcap,
                             w=b_tr + t + 1: (
                        e.wait_ge(self.s_tr.h, w),
                        e.tensor_copy(d, sr).then_inc(self.s_u1T.h, 1)))

                def plan_v(t):
                    sl = t % 2
                    w_ev2 = b_vev + t if t >= 1 else 0
                    w_u1T = b_ev + t + 1

                    def _vmm(e, t=t, sl=sl, w=v_wv, w2=w_ev2, wu=w_u1T,
                             first=(t == 0)):
                        if first:
                            e.wait_ge(self.s_ldw2.h, w)
                        e.wait_ge(self.s_u1T.h, wu)
                        if w2:
                            e.wait_ge(self.s_vev.h, w2)
                        for c in range(DC):
                            for nb in range(2):
                                ins = e.matmul(
                                    psv[sl][:, nb * 512:(nb + 1) * 512],
                                    u1T[:, c * S + t * 128: c * S + (t + 1) * 128],
                                    wvb[:, c * D + nb * 512: c * D + (nb + 1) * 512],
                                    start=(c == 0), stop=(c == DC - 1),
                                    skip_group_check=True)
                        ins.then_inc(self.s_vmm.h, 1)
                    self.add("tensor", _vmm)
                    v_vmm = self.s_vmm.inc()
                    dstap = (vsb[:, t * NH * 66:(t + 1) * NH * 66]
                             .rearrange("p (h c) -> p h c", c=66)[:, :, 0:64])
                    srcap = psv[sl][:, :].rearrange("p (h c) -> p h c", c=64)
                    wv_first = v_vset if t == 0 else 0
                    self.add("vector", lambda e, d=dstap, sr=srcap, w=v_vmm,
                             w2=wv_first: (
                        e.wait_ge(self.s_misc.h, w2) if w2 else None,
                        e.wait_ge(self.s_vmm.h, w),
                        e.tensor_copy(d, sr).then_inc(self.s_vev.h, 1)))
                    self.s_vev.inc()

                for t in range(NT):
                    st_load(t)
                    st_stats(t)
                    st_rstd(t)
                    if t >= 1:
                        st_apply(t - 1)
                        st_tr(t - 1)
                        st_evac(t - 1)
                    if t >= 2:
                        plan_v(t - 2)
                st_apply(NT - 1)
                st_tr(NT - 1)
                st_evac(NT - 1)
                plan_v(NT - 2)
                plan_v(NT - 1)
                u1T_done = self.s_u1T.n

            with ExitStack() as scb:
                wqkb = [self.sb(scb, f"wqk{s}_{i}", [128, DC * 128], BF16)
                        for i in range(2)]
                qks = [self.sb(scb, f"qks{s}_{i}", [128, 512], F32)
                       for i in range(2)]
                t1b = [self.sb(scb, f"t1b{s}_{i}", [128, 512], F32)
                       for i in range(2)]
                t2b = self.sb(scb, f"t2b{s}", [128, 512], F32)
                psq = self.psq
                self.qkT_base = self.s_qkT.n
                self.vev_base = b_vev
                b_qk = self.s_qk.n
                b_rope = self.s_rope.n
                b_qkT = self.s_qkT.n
                self.add("tensor", lambda e, w=u1T_done: e.wait_ge(self.s_u1T.h, w))
                for h in range(NH):
                    wsem = self.s_ldw if h % 2 == 0 else self.s_ldwb
                    w_q = b_qk + (h - 1) * NQB if h >= 2 else 0
                    w_self = wsem.n if h < 2 else 0
                    v_ldw = wsem.inc(16)
                    self.add("sync", lambda e, h=h, ws=wsem, w=w_q, wsf=w_self,
                             wu=u1T_done if h < 2 else 0: (
                        e.wait_ge(self.s_qk.h, w) if w else None,
                        e.wait_ge(ws.h, wsf) if wsf else None,
                        e.wait_ge(self.s_u1T.h, wu) if wu else None,
                        e.dma_start(out=wqkb[h % 2][:, :]
                                    .rearrange("d (c e) -> d c e", e=128),
                                    in_=self.wqk_b[h].rearrange("c d e -> d c e"))
                         .then_inc(ws.h, 16)))
                    for qb in range(NQB):
                        i = h * NQB + qb
                        w_ps = b_rope + i - 1 if i >= 2 else 0

                        wqsem = self.s_ldw if h % 2 == 0 else self.s_ldwb

                        def _qkmm(e, h=h, qb=qb, sl=i % 2, ws=wqsem,
                                  wl=v_ldw, wp=w_ps):
                            e.wait_ge(ws.h, wl)
                            if wp:
                                e.wait_ge(self.s_rope.h, wp)
                            for c in range(DC):
                                ins = e.matmul(
                                    psq[sl][:, :],
                                    wqkb[h % 2][:, c * 128:(c + 1) * 128],
                                    u1T[:, c * S + qb * 512: c * S + qb * 512 + 512],
                                    start=(c == 0), stop=(c == DC - 1))
                            ins.then_inc(self.s_qk.h, 1)
                        self.add("tensor", _qkmm)
                        v_qk = self.s_qk.inc()
                        w_gp = b_qkT + i - 1 if i >= 2 else 0
                        self.add("vector", lambda e, sl=i % 2, qb=qb, w=v_qk, w2=w_gp: (
                            e.wait_ge(self.s_qk.h, w),
                            e.wait_ge(self.s_qkT.h, w2) if w2 else None,
                            e.stream_shuffle(qks[sl][:, :], psq[sl][:, :], SHUF16),
                            e.tensor_mul(t1b[sl][:, :], psq[sl][:, :],
                                         self.ropeA[:, qb * 512:qb * 512 + 512])
                             .then_inc(self.s_rope.h, 1)))
                        v_rope = self.s_rope.inc()

                        c1 = self.s_chg.inc()
                        w_own = b_qkT + i if i >= 1 else 0

                        def _gp(e, h=h, qb=qb, sl=i % 2, w=v_rope, c1=c1,
                                wo=w_own):
                            e.wait_ge(self.s_rope.h, w)
                            if wo:
                                e.wait_ge(self.s_qkT.h, wo)
                            e.tensor_mul(t2b[:, :], qks[sl][:, :],
                                         self.ropeB[:, qb * 512:qb * 512 + 512]) \
                             .then_inc(self.s_chg.h, 1)
                            e.wait_ge(self.s_chg.h, c1)
                            e.tensor_add(qkT[:, h * S + qb * 512:
                                             h * S + qb * 512 + 512],
                                         t1b[sl][:, :], t2b[:, :]) \
                             .then_inc(self.s_qkT.h, 1)
                        self.add("gpsimd", _gp)
                        self.s_qkT.inc()

    def p2a(self, s, sc1, qkT, vsb, attnT):
        S, NQB = self.S, self.NQB
        with ExitStack() as sc:
            PT = self.sb(sc, f"PT{s}", [128, 8 * 512], BF16)
            lrow = [self.sb(sc, f"lr{s}_{i}", [1, 512], F32) for i in range(2)]
            rb = [self.sb(sc, f"rb{s}_{i}", [64, 512], F32) for i in range(2)]
            pss6 = self.ps(sc, f"pss{s}", [128, 3072], F32)
            pss = [pss6[:, i * 1024:(i + 1) * 1024] for i in range(3)]
            posb = [self.ps(sc, f"po{s}_{i}", [128, 512], F32) for i in range(2)]
            pso = posb

            self.add("tensor", lambda e, w=self.qkT_base + NH * self.NQB:
                     e.wait_ge(self.s_qkT.h, w))
            pending = None

            def plan_pv(h, qb, i_hqb, K, exp_done, msk_done, js):
                for j in js:
                    sigma = max(0, j - 4 * qb) * 128
                    sl4 = j % 8
                    wmask = msk_done.get(j, 0)
                    w_exp = exp_done[j // 2]

                    wv = self.vev_base + j + 1

                    def _pv(e, h=h, qb=qb, j=j, K=K, sl4=sl4, sigma=sigma,
                            we=w_exp, wm=wmask, i_hqb=i_hqb, wv=wv):
                        e.wait_ge(self.s_vev.h, wv)
                        e.wait_ge(self.s_exp.h, we)
                        if wm:
                            e.wait_ge(self.s_msk.h, wm)
                        if j == 0 and i_hqb >= 2:
                            e.wait_ge(self.s_norm.h, i_hqb - 1)
                        e.matmul(pso[i_hqb % 2][0:65, sigma:512],
                                 vsb[:, j * NH * 66 + h * 66:
                                     j * NH * 66 + h * 66 + 65],
                                 PT[:, sl4 * 512 + sigma: sl4 * 512 + 512],
                                 start=(j == 0), stop=(j == K - 1),
                                 skip_group_check=True) \
                         .then_inc(self.s_pvmm.h, 1)
                    self.add("tensor", _pv)
                    self.s_pvmm.inc()
                    self.pt_free[j % 8] = self.s_pvmm.n

            def plan_tail(h, qb, i_hqb):
                w_pv = self.s_pvmm.n
                self.add("scalar", lambda e, sl=i_hqb % 2, w=w_pv: (
                    e.wait_ge(self.s_pvmm.h, w),
                    e.copy(lrow[sl][:, :], pso[sl][64:65, :])
                     .then_inc(self.s_lrow.h, 1)))
                v_l = self.s_lrow.inc()
                w_rec = self.s_norm.n - 1 if i_hqb >= 2 else 0
                self.add("tensor", lambda e, sl=i_hqb % 2, w=v_l, w2=w_rec: (
                    e.wait_ge(self.s_lrow.h, w),
                    e.wait_ge(self.s_norm.h, w2) if w2 else None,
                    e.matmul(posb[sl][64:128, :], self.ones[0:1, 0:64],
                             lrow[sl][:, :], start=True, stop=True,
                             skip_group_check=True)
                     .then_inc(self.s_bc.h, 1)))
                v_bc = self.s_bc.inc()
                dst = attnT[(h % 2) * 64:(h % 2) * 64 + 64,
                            (h // 2) * S + qb * 512:(h // 2) * S + qb * 512 + 512]
                c1 = self.s_ch.inc()
                self.add("vector", lambda e, sl=i_hqb % 2, d=dst, w=v_bc, c1=c1: (
                    e.wait_ge(self.s_bc.h, w),
                    e.reciprocal(rb[sl][:, :], posb[sl][64:128, :])
                     .then_inc(self.s_ch.h, 1),
                    e.wait_ge(self.s_ch.h, c1),
                    e.tensor_mul(d, pso[sl][0:64, :], rb[sl][:, :])
                     .then_inc(self.s_norm.h, 1)))
                self.s_norm.inc()

            for h in range(NH):
                for qb in range(NQB):
                    K = 4 * qb + 4
                    i_hqb = self.ghqb
                    self.ghqb += 1
                    exp_done = {}
                    msk_done = {}
                    for pl0 in range(0, K // 2, 2):
                        group = []
                        for pl in (pl0, pl0 + 1):
                            j0, j1 = 2 * pl, 2 * pl + 1
                            gp = self.gpair
                            self.gpair += 1
                            sl2 = gp % 3
                            w_free = self.pss_free[sl2]
                            qcol = (2 * (h // 2)) * S
                            kcol = (2 * (h // 2) + 1) * S
                            hb = 64 * (h % 2)
                            w_head = 0

                            def _smm(e, qb=qb, j0=j0, sl2=sl2, wf=w_free,
                                     qcol=qcol, kcol=kcol, hb=hb, wh=w_head):
                                if wh:
                                    e.wait_ge(self.s_qkT.h, wh)
                                if wf:
                                    e.wait_ge(self.s_exp.h, wf)
                                for jj in (j0, j0 + 1):
                                    e.matmul(pss[sl2][:, (jj % 2) * 512:
                                                      (jj % 2) * 512 + 512],
                                             qkT[hb:hb + 64, kcol + jj * 128:
                                                 kcol + (jj + 1) * 128],
                                             qkT[hb:hb + 64, qcol + qb * 512:
                                                 qcol + qb * 512 + 512],
                                             start=True, stop=True) \
                                     .then_inc(self.s_smm.h, 1)
                            self.add("tensor", _smm)
                            self.s_smm.inc(2)
                            group.append((pl, j0, j1, sl2, self.s_smm.n))

                        if pending is not None:
                            for (ph, pqb, pi, pK, pexp, pmsk, pjs) in pending:
                                plan_pv(ph, pqb, pi, pK, pexp, pmsk, pjs)
                                if pjs[-1] == pK - 1:
                                    plan_tail(ph, pqb, pi)

                        newpend = []
                        for (pl, j0, j1, sl2, w_smm) in group:
                            slot0 = j0 % 8
                            w_pt = max(self.pt_free[slot0],
                                       self.pt_free[slot0 + 1])

                            def _exp(e, sl2=sl2, slot0=slot0, w=w_smm, wp=w_pt):
                                e.wait_ge(self.s_smm.h, w)
                                if wp:
                                    e.wait_ge(self.s_pvmm.h, wp)
                                e.activation(PT[:, slot0 * 512:
                                                slot0 * 512 + 1024],
                                             pss[sl2][:, :], AF.Exp,
                                             bias=self.ecol[:, 1:2],
                                             scale=0.125) \
                                 .then_inc(self.s_exp.h, 1)
                            self.add("scalar", _exp)
                            v_exp = self.s_exp.inc()
                            self.pss_free[sl2] = v_exp
                            exp_done[pl] = v_exp
                            for j in (j0, j1):
                                if j >= 4 * qb:
                                    sigma = (j - 4 * qb) * 128
                                    col = (j % 8) * 512 + sigma

                                    def _msk(e, col=col, w=v_exp):
                                        e.wait_ge(self.s_exp.h, w)
                                        e.affine_select(
                                            PT[:, col:col + 128],
                                            PT[:, col:col + 128],
                                            pattern=[[1, 128]],
                                            compare_op=ALU.is_ge, fill=0.0,
                                            base=0, channel_multiplier=-1) \
                                         .then_inc(self.s_msk.h, 1)
                                    self.add("gpsimd", _msk)
                                    msk_done[j] = self.s_msk.inc()
                            newpend.append((h, qb, i_hqb, K, exp_done,
                                            msk_done, [j0, j1]))
                        pending = newpend
            if pending is not None:
                for (ph, pqb, pi, pK, pexp, pmsk, pjs) in pending:
                    plan_pv(ph, pqb, pi, pK, pexp, pmsk, pjs)
                    if pjs[-1] == pK - 1:
                        plan_tail(ph, pqb, pi)
            self.attnT_done = self.s_norm.n

    def p2b(self, s, sc2, attnT, x1):
        S, NT = self.S, self.NT
        with ExitStack() as sc:
            wob = self.sb(sc, f"wob{s}", [128, DC * D], BF16)
            xr = [self.sb(sc, f"xr{s}_{i}", [128, D], F32) for i in range(2)]
            psw = [self.ps(sc, f"psw{s}_{i}", [128, 1024], F32) for i in range(2)]
            pv_all = self.s_pvmm.n
            for c in range(DC):
                self.add("sync", lambda e, c=c, wp=pv_all if c == 0 else 0: (
                    e.wait_ge(self.s_pvmm.h, wp) if wp else None,
                    e.dma_start(out=wob[:, c * D:(c + 1) * D], in_=self.wo_b[c])
                    .then_inc(self.s_ldw2.h, 16)))
                self.s_ldw2.inc(16)
            v_wo = self.s_ldw2.n
            b_x1 = self.s_x1.n
            for t in range(NT):
                sl = t % 2
                lsem = self.s_ldx if sl == 0 else self.s_ldxb
                w_x1 = b_x1 + t - 1 if t >= 2 else 0
                w_self = lsem.n if t < 2 else 0
                v_ldx = lsem.inc(16)
                self.add("sync", lambda e, s_=s, t=t, sl=sl, ls=lsem,
                         w=w_x1, wsf=w_self: (
                    e.wait_ge(self.s_x1.h, w) if w else None,
                    e.wait_ge(ls.h, wsf) if wsf else None,
                    e.dma_start(out=xr[sl][:, :],
                                in_=self.x_hbm[s_, t * 128:(t + 1) * 128, :])
                     .then_inc(ls.h, 16)))

                def _womm(e, t=t, sl=sl, w=v_wo, w2=w_x1,
                          wa=self.attnT_done, first=(t == 0)):
                    if first:
                        e.wait_ge(self.s_ldw2.h, w)
                        e.wait_ge(self.s_norm.h, wa)
                    if w2:
                        e.wait_ge(self.s_x1.h, w2)
                    for c in range(DC):
                        for nb in range(2):
                            ins = e.matmul(
                                psw[sl][:, nb * 512:(nb + 1) * 512],
                                attnT[:, c * S + t * 128: c * S + (t + 1) * 128],
                                wob[:, c * D + nb * 512: c * D + (nb + 1) * 512],
                                start=(c == 0), stop=(c == DC - 1),
                                skip_group_check=True)
                    ins.then_inc(self.s_womm.h, 1)
                self.add("tensor", _womm)
                v_w = self.s_womm.inc()
                self.add("vector", lambda e, t=t, sl=sl, ls=lsem, w=v_w,
                         wl=v_ldx: (
                    e.wait_ge(self.s_womm.h, w),
                    e.wait_ge(ls.h, wl),
                    e.tensor_add(x1[:, t * D:(t + 1) * D], psw[sl][:, :],
                                 xr[sl][:, :]).then_inc(self.s_x1.h, 1)))
                self.s_x1.inc()

    def p3(self, s, sc2, x1):
        S, NT = self.S, self.NT
        with ExitStack() as sc:
            zT = self.sb(sc, f"zT{s}", [128, DC * S], BF16)
            wdbuf = [self.sb(sc, f"wdb{s}_{i}", [128, DC * HE], BF16)
                     for i in range(2)]
            wupbuf = [self.sb(sc, f"wub{s}_{i}", [HE, D], BF16)
                      for i in range(2)]
            wdsel = self.sb(sc, f"wds{s}", [128, DC * HE], BF16)
            wupsel = self.sb(sc, f"wus{s}", [HE, D], BF16)
            bdsel = self.sb(sc, f"bds{s}", [HE, 1], F32)
            selbc = self.sb(sc, f"selbc{s}", [128, 16], F32)
            bg = self.sb(sc, f"bg{s}", [128, D], F32)
            hT = self.sb(sc, f"hT{s}", [64, S], BF16)
            u2b = [self.sb(sc, f"u2{s}_{i}", [128, D], F32) for i in range(2)]
            t3b = [self.sb(sc, f"t3{s}_{i}", [128, D], F32) for i in range(2)]
            zb = [self.sb(sc, f"zb{s}_{i}", [128, D], BF16) for i in range(2)]
            gz = self.sb(sc, f"gz{s}", [128, D], BF16)
            st6 = [self.sb(sc, f"st6b{s}_{i}", [128, 12], F32)
                   for i in range(2)]
            mvA = [self.sb(sc, f"mvA{s}_{i}", [128, 2], F32) for i in range(2)]
            mvB = [self.sb(sc, f"mvB{s}_{i}", [128, 2], F32) for i in range(2)]
            rsA = [self.sb(sc, f"rsA{s}_{i}", [128, 1], F32) for i in range(2)]
            rsB = [self.sb(sc, f"rsB{s}_{i}", [128, 1], F32) for i in range(2)]
            rows = self.sb(sc, f"rows{s}", [1, D + 64], F32)
            crow4 = self.sb(sc, f"crow4_{s}", [1, 4 * D], F32)
            x1_all = self.s_x1.n
            self.add("sync", lambda e, wp=x1_all: (
                e.wait_ge(self.s_x1.h, wp),
                e.dma_start(out=crow4[:, :], in_=self.din["crow"][:, 0:4 * D])
                .then_inc(self.s_ldw2.h, 16)))
            v_crow = self.s_ldw2.inc(16)
            ohgcol = self.sb(sc, f"ohg{s}", [NE, 1], F32)
            zbnT_t = self.sb(sc, f"zbnT{s}", [128, NE], F32)
            t2buf = [self.sb(sc, f"t2o{s}_{i}", [128, D], F32) for i in range(2)]
            x2buf = [self.sb(sc, f"x2o{s}_{i}", [128, D], F32) for i in range(2)]

            R_zs = rows[:, 0:D]
            R_zbn = R_zs
            R_bup = R_zs
            o3 = D
            R_lg = rows[:, o3:o3 + NE]
            R_prob = rows[:, o3 + NE:o3 + 2 * NE]
            R_oh = rows[:, o3 + 2 * NE:o3 + 3 * NE]
            R_ohg = rows[:, o3 + 3 * NE:o3 + 4 * NE]
            nmx = rows[:, o3 + 4 * NE:o3 + 4 * NE + 1]
            sume = rows[:, o3 + 4 * NE + 1:o3 + 4 * NE + 2]
            rsum = rows[:, o3 + 4 * NE + 2:o3 + 4 * NE + 3]
            gate = rows[:, o3 + 4 * NE + 3:o3 + 4 * NE + 4]


            with ExitStack() as sa:
                psz = [self.ps(sa, f"psz{s}_{i}", [1, 512], F32) for i in range(2)]
                psztr = [self.ps(sa, f"pzt{s}_{i}", [128, 1024], BF16)
                         for i in range(2)]
                b_zmm = self.s_zmm.n
                b_z = self.s_z.n
                b_ztr = self.s_ztr.n
                b_zT = self.s_zT.n
                b_st3 = self.s_stat.n
                b_rs3 = self.s_rstd.n
                b_t3 = self.s_t3.n

                V1, V2, VR1, VR2 = {}, {}, {}, {}

                def q_s1(t):
                    sl = t % 2
                    c0 = self.s_ch.inc()
                    wp = V2.get(t - 2, 0)
                    V1[t] = self.s_stat.inc()
                    self.add("vector", lambda e, t=t, sl=sl, c0=c0, wp=wp: (
                        e.wait_ge(self.s_stat.h, wp) if wp else None,
                        e.bn_stats(st6[sl][:, 0:6], x1[:, t * D:t * D + 512]),
                        e.bn_stats(st6[sl][:, 6:12], x1[:, t * D + 512:(t + 1) * D])
                         .then_inc(self.s_ch.h, 1),
                        e.wait_ge(self.s_ch.h, c0),
                        e.bn_aggr(mvA[sl][:, :], st6[sl][:, :])
                         .then_inc(self.s_stat.h, 1)))

                def q_r1(t):
                    sl = t % 2
                    c1 = self.s_cha.inc()
                    VR1[t] = self.s_rstd.inc()
                    self.add("scalar", lambda e, sl=sl, w=V1[t],
                             c1=c1: (
                        e.wait_ge(self.s_stat.h, w),
                        e.activation(rsA[sl][:, :], mvA[sl][:, 1:2], AF.Ln,
                                     bias=self.ecol[:, 0:1])
                         .then_inc(self.s_cha.h, 1),
                        e.wait_ge(self.s_cha.h, c1),
                        e.activation(rsA[sl][:, :], rsA[sl][:, :], AF.Exp,
                                     scale=-0.5).then_inc(self.s_rstd.h, 1)))

                def q_s2(t):
                    sl = t % 2
                    c1 = self.s_ch.inc()
                    c2 = self.s_ch.inc()
                    V2[t] = self.s_stat.inc()
                    self.add("vector", lambda e, t=t, sl=sl, w=VR1[t],
                             c1=c1, c2=c2: (
                        e.wait_ge(self.s_rstd.h, w),
                        e.tensor_scalar(u2b[sl][:, :], x1[:, t * D:(t + 1) * D],
                                        mvA[sl][:, 0:1], rsA[sl][:, 0:1],
                                        ALU.subtract, ALU.mult)
                         .then_inc(self.s_ch.h, 1),
                        e.wait_ge(self.s_ch.h, c1),
                        e.bn_stats(st6[sl][:, 0:6], u2b[sl][:, 0:512]),
                        e.bn_stats(st6[sl][:, 6:12], u2b[sl][:, 512:1024])
                         .then_inc(self.s_ch.h, 1),
                        e.wait_ge(self.s_ch.h, c2),
                        e.bn_aggr(mvB[sl][:, :], st6[sl][:, :])
                         .then_inc(self.s_stat.h, 1)))

                def q_r2(t):
                    sl = t % 2
                    c1 = self.s_cha.inc()
                    VR2[t] = self.s_rstd.inc()
                    self.add("scalar", lambda e, sl=sl, w=V2[t],
                             c1=c1: (
                        e.wait_ge(self.s_stat.h, w),
                        e.activation(rsB[sl][:, :], mvB[sl][:, 1:2], AF.Ln,
                                     bias=self.ecol[:, 0:1])
                         .then_inc(self.s_cha.h, 1),
                        e.wait_ge(self.s_cha.h, c1),
                        e.activation(rsB[sl][:, :], rsB[sl][:, :], AF.Exp,
                                     scale=-0.5).then_inc(self.s_rstd.h, 1)))

                def q_t3(t):
                    sl = t % 2
                    w_zmm = b_zmm + t - 1 if t >= 2 else 0
                    w_z = b_z + t - 1 if t >= 2 else 0
                    self.s_t3.inc()
                    self.add("vector", lambda e, sl=sl, w=VR2[t],
                             w2=w_zmm, w3=w_z: (
                        e.wait_ge(self.s_rstd.h, w),
                        e.wait_ge(self.s_zmm.h, w2) if w2 else None,
                        e.wait_ge(self.s_z.h, w3) if w3 else None,
                        e.tensor_scalar(t3b[sl][:, :], u2b[sl][:, :],
                                        mvB[sl][:, 0:1], rsB[sl][:, 0:1],
                                        ALU.subtract, ALU.mult)
                         .then_inc(self.s_t3.h, 1)))

                def q_zmm(t):
                    sl = t % 2

                    def _zmm(e, t=t, sl=sl, w=b_t3 + t + 1):
                        e.wait_ge(self.s_t3.h, w)
                        for nb in range(2):
                            ins = e.matmul(psz[nb][:, :], self.ones[:, 0:1],
                                           t3b[sl][:, nb * 512:(nb + 1) * 512],
                                           start=(t == 0), stop=(t == NT - 1),
                                           skip_group_check=True)
                        ins.then_inc(self.s_zmm.h, 1)
                    self.add("tensor", _zmm)
                    self.s_zmm.inc()

                def q_z(t):
                    sl = t % 2
                    w_ztr = b_ztr + t - 1 if t >= 2 else 0
                    c1 = self.s_chg.inc()
                    w_own = b_z + t if t >= 1 else 0
                    self.add("gpsimd", lambda e, sl=sl, w=b_t3 + t + 1,
                             w2=w_ztr, c1=c1, wo=w_own: (
                        e.wait_ge(self.s_t3.h, w),
                        e.wait_ge(self.s_ztr.h, w2) if w2 else None,
                        e.wait_ge(self.s_z.h, wo) if wo else None,
                        e.tensor_mul(gz[:, :], t3b[sl][:, :], self.gtile[:, :])
                         .then_inc(self.s_chg.h, 1),
                        e.wait_ge(self.s_chg.h, c1),
                        e.tensor_add(zb[sl][:, :], gz[:, :], self.btile[:, :])
                         .then_inc(self.s_z.h, 1)))
                    self.s_z.inc()

                def q_ztr(t):
                    sl = t % 2
                    w_zT = b_zT + t - 1 if t >= 2 else 0

                    def _ztr(e, sl=sl, w=b_z + t + 1, w2=w_zT):
                        e.wait_ge(self.s_z.h, w)
                        if w2:
                            e.wait_ge(self.s_zT.h, w2)
                        for c in range(DC):
                            ins = e.transpose(psztr[sl][:, c * 128:(c + 1) * 128],
                                              zb[sl][:, c * 128:(c + 1) * 128],
                                              self.ident[:, 0:128])
                        ins.then_inc(self.s_ztr.h, 1)
                    self.add("tensor", _ztr)
                    self.s_ztr.inc()
                    dstap = (zT[:, :].rearrange("p (c q) -> p c q", q=S)
                             [:, :, t * 128:(t + 1) * 128])
                    srcap = psztr[sl][:, :].rearrange("p (c f) -> p c f", f=128)
                    self.add("vector", lambda e, d=dstap, sr=srcap,
                             w=b_ztr + t + 1: (
                        e.wait_ge(self.s_ztr.h, w),
                        e.tensor_copy(d, sr).then_inc(self.s_zT.h, 1)))
                    self.s_zT.inc()

                for t in range(NT):
                    q_s1(t)
                    q_r1(t)
                    if t >= 1:
                        q_s2(t - 1)
                        q_r2(t - 1)
                    if t >= 2:
                        q_t3(t - 2)
                        q_zmm(t - 2)
                        q_z(t - 2)
                        q_ztr(t - 2)
                q_s2(NT - 1)
                q_r2(NT - 1)
                for t in (NT - 2, NT - 1):
                    q_t3(t)
                    q_zmm(t)
                    q_z(t)
                    q_ztr(t)
                zT_done = self.s_zT.n
                zmm_done = self.s_zmm.n
                self.add("scalar", lambda e, w=zmm_done: (
                    e.wait_ge(self.s_zmm.h, w),
                    e.copy(R_zs[:, 0:512], psz[0][:, :]),
                    e.copy(R_zs[:, 512:1024], psz[1][:, :])
                     .then_inc(self.s_gat.h, 1)))
                v_g = self.s_gat.inc()

            with ExitStack() as sg:
                pzb = self.ps(sg, f"pzb{s}", [128, NE], F32)
                plg = self.ps(sg, f"plg{s}", [1, NE], F32)
                pohbc = self.ps(sg, f"poh{s}", [128, 16], F32)
                pbup = [self.ps(sg, f"pbu{s}_{i}", [1, 512], F32) for i in range(2)]
                pbg = self.ps(sg, f"pbg{s}", [128, 1024], F32)
                pohg = self.ps(sg, f"pog{s}", [NE, 1], F32)

                cb = self.s_ch.n
                self.s_ch.inc(5)

                def _zbn(e, w=v_g, cb=cb, wc=v_crow):
                    e.wait_ge(self.s_gat.h, w)
                    e.wait_ge(self.s_ldw2.h, wc)
                    e.tensor_scalar_mul(R_zs, R_zs, 1.0 / S).then_inc(self.s_ch.h, 1)
                    e.wait_ge(self.s_ch.h, cb + 1)
                    e.tensor_mul(R_zs, R_zs, crow4[0:1, 2 * D:3 * D]) \
                     .then_inc(self.s_ch.h, 1)
                    e.wait_ge(self.s_ch.h, cb + 2)
                    e.tensor_add(R_zs, R_zs, crow4[0:1, 3 * D:4 * D]) \
                     .then_inc(self.s_ch.h, 1)
                    e.wait_ge(self.s_ch.h, cb + 3)
                    e.tensor_sub(R_zbn, R_zs, crow4[0:1, 0:D]) \
                     .then_inc(self.s_ch.h, 1)
                    e.wait_ge(self.s_ch.h, cb + 4)
                    e.tensor_mul(R_zbn, R_zbn, self.alpha[0:1, :]) \
                     .then_inc(self.s_ch.h, 1)
                    e.wait_ge(self.s_ch.h, cb + 5)
                    e.tensor_add(R_zbn, R_zbn, crow4[0:1, D:2 * D]) \
                     .then_inc(self.s_gat.h, 1)
                self.add("vector", _zbn)
                v_g = self.s_gat.inc()

                def _zbnT(e, w=v_g):
                    e.wait_ge(self.s_gat.h, w)
                    for c in range(DC):
                        ins = e.transpose(pzb[:, c:c + 1],
                                          R_zbn[:, c * 128:(c + 1) * 128],
                                          self.ones[0:1, 0:1])
                    ins.then_inc(self.s_gat.h, 1)
                self.add("tensor", _zbnT)
                v_g = self.s_gat.inc()
                self.add("scalar", lambda e, w=v_g: (
                    e.wait_ge(self.s_gat.h, w),
                    e.copy(zbnT_t[:, :], pzb[:, :]).then_inc(self.s_gat.h, 1)))
                v_g = self.s_gat.inc()

                def _lg(e, w=v_g):
                    e.wait_ge(self.s_gat.h, w)
                    for c in range(DC):
                        ins = e.matmul(plg[:, :], zbnT_t[:, c:c + 1],
                                       self.wrsb[:, c * NE:(c + 1) * NE],
                                       start=(c == 0), stop=(c == DC - 1))
                    ins.then_inc(self.s_gat.h, 1)
                self.add("tensor", _lg)
                v_g = self.s_gat.inc()
                self.add("scalar", lambda e, w=v_g: (
                    e.wait_ge(self.s_gat.h, w),
                    e.copy(R_lg, plg[:, :]).then_inc(self.s_gat.h, 1)))
                v_g = self.s_gat.inc()

                def _sm1(e, w=v_g, c1=self.s_ch.inc()):
                    e.wait_ge(self.s_gat.h, w)
                    e.tensor_add(R_lg, R_lg, self.brsb[:, :]) \
                     .then_inc(self.s_ch.h, 1)
                    e.wait_ge(self.s_ch.h, c1)
                    e.reduce_max(nmx, R_lg, axis=AX.X, negate=True) \
                     .then_inc(self.s_gat.h, 1)
                self.add("vector", _sm1)
                v_g = self.s_gat.inc()
                self.add("scalar", lambda e, w=v_g: (
                    e.wait_ge(self.s_gat.h, w),
                    e.activation(R_prob, R_lg, AF.Exp, bias=nmx,
                                 accum_out=sume).then_inc(self.s_gat.h, 1)))
                v_g = self.s_gat.inc()

                def _sm2(e, w=v_g):
                    e.wait_ge(self.s_gat.h, w)
                    e.reciprocal(rsum, sume).then_inc(self.s_gat.h, 1)
                self.add("vector", _sm2)
                v_g = self.s_gat.inc()

                def _sm3(e, w=v_g):
                    e.wait_ge(self.s_gat.h, w)
                    e.tensor_single_scalar(R_prob, R_prob, rsum, ALU.mult) \
                     .then_inc(self.s_gat.h, 1)
                self.add("vector", _sm3)
                v_g = self.s_gat.inc()

                def _sm4(e, w=v_g):
                    e.wait_ge(self.s_gat.h, w)
                    e.reduce_max(gate, R_prob, axis=AX.X).then_inc(self.s_gat.h, 1)
                self.add("vector", _sm4)
                v_g = self.s_gat.inc()

                def _sm5(e, w=v_g):
                    e.wait_ge(self.s_gat.h, w)
                    e.tensor_single_scalar(R_oh, R_prob, gate, ALU.is_equal) \
                     .then_inc(self.s_gat.h, 1)
                self.add("vector", _sm5)
                v_g = self.s_gat.inc()

                def _sm6(e, w=v_g):
                    e.wait_ge(self.s_gat.h, w)
                    e.tensor_single_scalar(R_ohg, R_oh, gate, ALU.mult) \
                     .then_inc(self.s_gat.h, 1)
                self.add("vector", _sm6)
                v_g = self.s_gat.inc()

                def _bc(e, w=v_g):
                    e.wait_ge(self.s_gat.h, w)
                    e.matmul(pohbc[:, 0:NE], self.ones[0:1, :], R_oh,
                             start=True, stop=True)
                    e.matmul(pohbc[:, NE:2 * NE], self.ones[0:1, :], R_ohg,
                             start=True, stop=True, skip_group_check=True)
                    e.transpose(pohg[:, :], R_ohg, self.ones[0:1, 0:1]) \
                     .then_inc(self.s_gat.h, 1)
                self.add("tensor", _bc)
                v_g = self.s_gat.inc()
                self.add("scalar", lambda e, w=v_g: (
                    e.wait_ge(self.s_gat.h, w),
                    e.copy(selbc[:, :], pohbc[:, :]),
                    e.copy(ohgcol[:, :], pohg[:, :]).then_inc(self.s_gat.h, 1)))
                v_g = self.s_gat.inc()

                def _bup(e, w=v_g):
                    e.wait_ge(self.s_gat.h, w)
                    e.matmul(pbup[0][:, :], ohgcol[:, :],
                             self.bupall[:, 0:512], start=True, stop=True)
                    e.matmul(pbup[1][:, :], ohgcol[:, :],
                             self.bupall[:, 512:1024], start=True,
                             stop=True, skip_group_check=True) \
                     .then_inc(self.s_gat.h, 1)
                self.add("tensor", _bup)
                v_g = self.s_gat.inc()
                self.add("scalar", lambda e, w=v_g: (
                    e.wait_ge(self.s_gat.h, w),
                    e.copy(R_bup[:, 0:512], pbup[0][:, :]),
                    e.copy(R_bup[:, 512:1024], pbup[1][:, :])
                     .then_inc(self.s_gat.h, 1)))
                v_g = self.s_gat.inc()

                def _bg(e, w=v_g):
                    e.wait_ge(self.s_gat.h, w)
                    for nb in range(2):
                        ins = e.matmul(pbg[:, nb * 512:(nb + 1) * 512],
                                       self.ones[0:1, :],
                                       R_bup[:, nb * 512:(nb + 1) * 512],
                                       start=True, stop=True,
                                       skip_group_check=True)
                    ins.then_inc(self.s_gat.h, 1)
                self.add("tensor", _bg)
                v_g = self.s_gat.inc()
                self.add("vector", lambda e, w=v_g: (
                    e.wait_ge(self.s_gat.h, w),
                    e.tensor_copy(bg[:, :], pbg[:, :]).then_inc(self.s_gat.h, 1)))
                v_bg = self.s_gat.inc()

                b_sel = self.s_sel.n
                for ei in range(NE):
                    sl = ei % 2
                    esem = self.s_ldw3 if sl == 0 else self.s_ldw3b
                    w_gp = b_sel + ei - 1 if ei >= 2 else 0
                    w_self = esem.n if ei < 2 else 0
                    v_lw = esem.inc(32)
                    self.add("sync", lambda e, ei=ei, sl=sl, es=esem, w=w_gp,
                             wsf=w_self, wx=x1_all if ei < 2 else 0: (
                        e.wait_ge(self.s_sel.h, w) if w else None,
                        e.wait_ge(es.h, wsf) if wsf else None,
                        e.wait_ge(self.s_x1.h, wx) if wx else None,
                        e.dma_start(out=wdbuf[sl][:, :],
                                    in_=self.wd_b[:, ei * DC * HE:
                                                  (ei + 1) * DC * HE])
                         .then_inc(es.h, 16),
                        e.dma_start(out=wupbuf[sl][:, :],
                                    in_=self.wup_b[:, ei * D:(ei + 1) * D])
                         .then_inc(es.h, 16)))

                    def _sel(e, ei=ei, sl=sl, es=esem, w=v_lw, wg=v_bg,
                             cprev=self.s_sel.n):
                        if ei == 0:
                            e.wait_ge(self.s_gat.h, wg)
                        else:
                            e.wait_ge(self.s_sel.h, cprev)
                        e.wait_ge(es.h, w)
                        if ei == 0:
                            e.tensor_scalar(wdsel[:, :], wdbuf[sl][:, :],
                                            selbc[:, 0:1], None, ALU.mult)
                            e.tensor_scalar(bdsel[:, :], self.bdall[:, 0:1],
                                            selbc[0:HE, 0:1], None, ALU.mult)
                            ins = e.tensor_scalar(
                                wupsel[:, :], wupbuf[sl][:, :],
                                selbc[0:HE, NE:NE + 1], None, ALU.mult)
                        else:
                            e.scalar_tensor_tensor(
                                wdsel[:, :], wdbuf[sl][:, :],
                                selbc[:, ei:ei + 1], wdsel[:, :],
                                ALU.mult, ALU.add)
                            e.scalar_tensor_tensor(
                                bdsel[:, :], self.bdall[:, ei:ei + 1],
                                selbc[0:HE, ei:ei + 1], bdsel[:, :],
                                ALU.mult, ALU.add)
                            ins = e.scalar_tensor_tensor(
                                wupsel[:, :], wupbuf[sl][:, :],
                                selbc[0:HE, NE + ei:NE + ei + 1],
                                wupsel[:, :], ALU.mult, ALU.add)
                        ins.then_inc(self.s_sel.h, 1)
                    self.add("vector", _sel)
                    self.s_sel.inc()
                v_sel = self.s_sel.n

            with ExitStack() as sy:
                psh = [self.ps(sy, f"psh{s}_{i}", [HE, 512], F32)
                       for i in range(2)]
                psy = [self.ps(sy, f"psy{s}_{i}", [128, 1024], F32)
                       for i in range(2)]
                NB = S // 512
                b_hr = self.s_hrelu.n
                for tb in range(NB):
                    sl = tb % 2
                    w_hr = b_hr + tb - 1 if tb >= 2 else 0

                    def _hmm(e, tb=tb, sl=sl, w=v_sel, w2=w_hr, wz=zT_done,
                             first=(tb == 0)):
                        if first:
                            e.wait_ge(self.s_sel.h, w)
                            e.wait_ge(self.s_zT.h, wz)
                        if w2:
                            e.wait_ge(self.s_hrelu.h, w2)
                        for c in range(DC):
                            ins = e.matmul(
                                psh[sl][:, :],
                                wdsel[:, c * HE:(c + 1) * HE],
                                zT[:, c * S + tb * 512: c * S + tb * 512 + 512],
                                start=(c == 0), stop=(c == DC - 1))
                        ins.then_inc(self.s_ymm.h, 1)
                    self.add("tensor", _hmm)
                    v_h = self.s_ymm.inc()
                    self.add("scalar", lambda e, tb=tb, sl=sl, w=v_h, ws=v_sel: (
                        e.wait_ge(self.s_ymm.h, w),
                        e.wait_ge(self.s_sel.h, ws),
                        e.activation(hT[:, tb * 512:(tb + 1) * 512],
                                     psh[sl][:, :], AF.Relu,
                                     bias=bdsel[:, 0:1])
                         .then_inc(self.s_hrelu.h, 1)))
                    self.s_hrelu.inc()
                b_t2 = self.s_t2.n
                b_x2 = self.s_x2.n
                for t in range(NT):
                    sl = t % 2
                    w_hr = b_hr + t // 4 + 1
                    w_t2 = b_t2 + t - 1 if t >= 2 else 0

                    def _ymm(e, t=t, sl=sl, w=w_hr, w2=w_t2):
                        e.wait_ge(self.s_hrelu.h, w)
                        if w2:
                            e.wait_ge(self.s_t2.h, w2)
                        for nb in range(2):
                            ins = e.matmul(
                                psy[sl][:, nb * 512:(nb + 1) * 512],
                                hT[:, t * 128:(t + 1) * 128],
                                wupsel[:, nb * 512:(nb + 1) * 512],
                                start=True, stop=True, skip_group_check=True)
                        ins.then_inc(self.s_ymm.h, 1)
                    self.add("tensor", _ymm)
                    v_y = self.s_ymm.inc()
                    w_x2 = b_x2 + t - 1 if t >= 2 else 0
                    self.add("vector", lambda e, sl=sl, w=v_y, wb=v_bg, w2=w_x2: (
                        e.wait_ge(self.s_ymm.h, w),
                        e.wait_ge(self.s_gat.h, wb),
                        e.wait_ge(self.s_x2.h, w2) if w2 else None,
                        e.tensor_add(t2buf[sl][:, :], psy[sl][:, :], bg[:, :])
                         .then_inc(self.s_t2.h, 1)))
                    v_t2 = self.s_t2.inc()
                    ssem = self.s_st if t % 2 == 0 else self.s_st2
                    w_st = ssem.n if t >= 2 else 0
                    self.add("gpsimd", lambda e, t=t, sl=sl, ss=ssem, w=v_t2,
                             w2=w_st: (
                        e.wait_ge(self.s_t2.h, w),
                        e.wait_ge(ss.h, w2) if w2 else None,
                        e.tensor_add(x2buf[sl][:, :], t2buf[sl][:, :],
                                     x1[:, t * D:(t + 1) * D])
                         .then_inc(self.s_x2.h, 1)))
                    v_x2 = self.s_x2.inc()
                    self.add("sync", lambda e, s_=s, t=t, sl=sl, ss=ssem, w=v_x2: (
                        e.wait_ge(self.s_x2.h, w),
                        e.dma_start(out=self.out_hbm[s_, t * 128:(t + 1) * 128, :],
                                    in_=x2buf[sl][:, :])
                         .then_inc(ss.h, 16)))
                    ssem.inc(16)


def _host_inputs(xs, W_qkv, W_o, ln_g, ln_b, bn_g, bn_b, bn_mean, bn_var,
                 Wr, br, W_down, b_down, W_up, b_up, S):
    d = {}
    d["x"] = np.ascontiguousarray(xs, dtype=np.float32)

    wqk = np.empty((NH, DC, 128, 128), np.float32)
    for i in range(NH // 2):
        qrows = np.concatenate([W_qkv[(2 * i) * 64 + PERM64],
                                W_qkv[(2 * i + 1) * 64 + PERM64]], axis=0)
        krows = np.concatenate([W_qkv[D + (2 * i) * 64 + PERM64],
                                W_qkv[D + (2 * i + 1) * 64 + PERM64]], axis=0)
        for c in range(DC):
            wqk[2 * i, c] = qrows[:, c * 128:(c + 1) * 128].T
            wqk[2 * i + 1, c] = krows[:, c * 128:(c + 1) * 128].T
    d["wqk"] = wqk

    wv = np.empty((DC, 128, D), np.float32)
    wo = np.empty((DC, 128, D), np.float32)
    for c in range(DC):
        wv[c] = W_qkv[2 * D:3 * D, c * 128:(c + 1) * 128].T
        wo[c] = W_o[:, c * 128:(c + 1) * 128].T
    d["wv"] = wv
    d["wo"] = wo

    wd = np.transpose(W_down.reshape(NE, DC, 128, HE), (2, 0, 1, 3))
    d["wd"] = np.ascontiguousarray(wd.reshape(128, NE * DC * HE), np.float32)
    d["wup"] = np.ascontiguousarray(
        np.transpose(W_up, (1, 0, 2)).reshape(HE, NE * D), np.float32)

    pos = np.arange(S, dtype=np.float64)
    inv = THETA ** (-np.arange(0, DK, 2, dtype=np.float64) / DK)
    A64 = np.empty((64, S), np.float64)
    B64 = np.empty((64, S), np.float64)
    for blk in range(2):
        pair = np.arange(16 * blk, 16 * blk + 16)
        ang = pos[None, :] * inv[pair][:, None]
        A64[blk * 32:blk * 32 + 16] = np.cos(ang)
        A64[blk * 32 + 16:blk * 32 + 32] = np.cos(ang)
        B64[blk * 32:blk * 32 + 16] = -np.sin(ang)
        B64[blk * 32 + 16:blk * 32 + 32] = np.sin(ang)
    d["ropeA"] = np.tile(A64, (2, 1)).astype(np.float32)
    d["ropeB"] = np.tile(B64, (2, 1)).astype(np.float32)

    d["gtile"] = np.tile(ln_g[None, :], (128, 1)).astype(np.float32)
    d["btile"] = np.tile(ln_b[None, :], (128, 1)).astype(np.float32)
    import ml_dtypes
    d["ident"] = np.concatenate([np.eye(128), np.zeros((128, 512))],
                                axis=1).astype(ml_dtypes.bfloat16)
    d["ones"] = np.ones((128, 128), np.float32)
    d["crow"] = np.concatenate(
        [bn_mean, bn_b, ln_g, ln_b, bn_var, bn_g]).reshape(1, 6 * D).astype(np.float32)
    d["wr"] = np.ascontiguousarray(
        np.transpose(Wr.reshape(DC, 128, NE), (1, 0, 2)).reshape(128, DC * NE),
        np.float32)
    d["br"] = br.reshape(1, NE).astype(np.float32)
    d["bdall"] = np.ascontiguousarray(b_down.T, np.float32)
    d["bupall"] = np.ascontiguousarray(b_up, np.float32)
    d["ecol"] = np.tile(np.array([[EPS, -CSHIFT]], np.float32), (128, 1))
    return d


def kernel(L0_attn_x, L0_attn_W_qkv, L0_attn_W_o, L0_ffn_ln_g, L0_ffn_ln_b,
           L0_ffn_bn_g, L0_ffn_bn_b, L0_ffn_bn_mean, L0_ffn_bn_var,
           L0_ffn_Wr, L0_ffn_br, L0_ffn_W_down, L0_ffn_b_down, L0_ffn_W_up,
           L0_ffn_b_up):
    Bfull, S, _ = L0_attn_x.shape
    B_core = Bfull // NCORES
    b = Builder(B_core, S)
    nc = b.build()
    in_maps = []
    for core in range(NCORES):
        xs = L0_attn_x[core * B_core:(core + 1) * B_core]
        in_maps.append(_host_inputs(
            xs, L0_attn_W_qkv, L0_attn_W_o, L0_ffn_ln_g, L0_ffn_ln_b,
            L0_ffn_bn_g, L0_ffn_bn_b, L0_ffn_bn_mean, L0_ffn_bn_var,
            L0_ffn_Wr, L0_ffn_br, L0_ffn_W_down, L0_ffn_b_down, L0_ffn_W_up,
            L0_ffn_b_up, S))
    res = run_bass_kernel_spmd(nc, in_maps, core_ids=list(range(NCORES)))
    out = np.concatenate([r["out"] for r in res.results], axis=0)
    return out.astype(np.float32)
